# version 20
# baseline (speedup 1.0000x reference)
import contextlib
import numpy as np

import concourse.bass as bass
import concourse.mybir as mybir
from concourse.bass_utils import run_bass_kernel_spmd

F32 = mybir.dt.float32
BF16 = mybir.dt.bfloat16
AF = mybir.ActivationFunctionType
ALU = mybir.AluOpType
AX = mybir.AxisListType

P = 128
D = 1024
DC = D // P
H = 16
HD = 64
FF = 4096
FCN = FF // P
EPS = 1e-5


class Sched:
    ENGINES = ("sync", "gpsimd", "vector", "scalar", "tensor")

    def __init__(self):
        self.q = {e: [] for e in self.ENGINES}
        import collections
        self.cnt = collections.defaultdict(int)
        self.lastw = {e: {} for e in self.ENGINES}

    def op(self, eng, meth, *args, waits=(), inc=None, amt=1, **kw):
        flat = []
        for w in waits:
            if w is None:
                continue
            if isinstance(w, list):
                flat.extend(x for x in w if x is not None)
            else:
                flat.append(w)
        fws = []
        for s, val in flat:
            if self.lastw[eng].get(s, 0) >= val:
                continue
            self.lastw[eng][s] = val
            fws.append((s, val))
        ev = None
        if inc is not None:
            self.cnt[inc] += amt
            ev = (inc, self.cnt[inc])
        self.q[eng].append((fws, meth, args, kw, inc, amt))
        return ev

    def replay(self, eng, h, sems):
        drains = eng in ("vector", "scalar")
        first = True
        for fws, meth, args, kw, inc, amt in self.q[eng]:
            if drains and not first:
                h.drain()
            first = False
            for s, v in fws:
                h.wait_ge(sems[s], v)
            ins = getattr(h, meth)(*args, **kw)
            if inc is not None:
                ins.then_inc(sems[inc], amt)


def build_nc(B, T):
    TPB = T // P
    NT = B * TPB
    NSUP = NT // 2
    assert T % 256 == 0

    nc = bass.Bass("TRN2", target_bir_lowering=False)

    x_d = nc.dram_tensor("x", [B, T, D], F32, kind="ExternalInput").ap()
    ident_d = nc.dram_tensor("ident", [P, P], F32, kind="ExternalInput").ap()
    w_d = {}
    for nm, shp in [("Wq", [D, D]), ("bq", [D]), ("Wk", [D, D]), ("bk", [D]),
                    ("Wv", [D, D]), ("bv", [D]), ("Wh", [D, D]), ("bh", [D]),
                    ("gamma", [D]), ("beta", [D]),
                    ("W1", [FF, D]), ("b1", [FF]), ("W2", [D, FF]), ("b2", [D])]:
        w_d[nm] = nc.dram_tensor(nm, shp, F32, kind="ExternalInput").ap()
    out_d = nc.dram_tensor("out", [B, T, D], F32, kind="ExternalOutput").ap()
    x2_d = nc.dram_tensor("x2scr", [B, T, D], F32, kind="Internal").ap()
    row_d = nc.dram_tensor("rowscr", [D], F32, kind="Internal").ap()

    sd = Sched()
    guards = {}

    def need(key):
        return guards.get(key)

    def protect(key, ev):
        guards[key] = ev if isinstance(ev, list) else [ev]

    def dma(eng, sem, out, in_, waits=(), serial=False, **kw):
        w = list(waits)
        if serial:
            w.append((sem, sd.cnt[sem]))
        return sd.op(eng, "dma_start", out=out, in_=in_, waits=w,
                     inc=sem, amt=16, **kw)

    with contextlib.ExitStack() as es:
        def sbuf(name, shape, dt):
            return es.enter_context(nc.sbuf_tensor(name, shape, dt))

        AW = sbuf("AW", [P, 65536], BF16)
        AR = [sbuf(f"ar{i}", [P, 1024], F32) for i in range(9)]
        A16 = sbuf("a16", [P, 4096], BF16)
        A_sb = sbuf("A_sb", [P, B * 2048], BF16)
        biasrows = sbuf("biasrows", [P, 4096], BF16)
        b1c = sbuf("b1c", [P, FCN], F32)
        B16 = sbuf("B16", [P, 4096], BF16)
        id32 = sbuf("id32", [P, P], F32)
        ones16 = sbuf("ones16", [P, 256], BF16)
        ones32 = sbuf("ones32", [P, P], F32)
        cb16 = sbuf("cb16", [1, 4], BF16)
        sm = sbuf("sm", [P, 152], F32)
        st_sc = sbuf("st_sc", [P, NT], F32)
        st_bi = sbuf("st_bi", [P, NT], F32)

        P_K = es.enter_context(nc.psum_tensor("P_K", [P, 1024], F32))
        P_V = es.enter_context(nc.psum_tensor("P_V", [P, 1024], F32))
        P_A = es.enter_context(nc.psum_tensor("P_A", [P, 512], F32))
        P_SP = es.enter_context(nc.psum_tensor("P_SP", [P, 512], F32))
        P_T0 = es.enter_context(nc.psum_tensor("P_T0", [P, 512], F32))
        P_T1 = es.enter_context(nc.psum_tensor("P_T1", [P, 512], F32))
        PT = (P_T0, P_T1)
        P_O = (P_K, P_V)
        GSL = (P_A, P_SP, P_T0, P_T1)

        sems = {}
        for s in ("lc", "lw0", "lw1", "lg", "lx0", "lx1", "l20", "l21", "l22",
                  "l23", "sx0", "sx1", "so0", "so1", "sr", "pe", "v", "s", "g"):
            sems[s] = es.enter_context(nc.semaphore(s))

        nxT_hi = A16[:, 0:1024]
        nxT_lo = A16[:, 1024:2048]
        xT_hi = A16[:, 2048:3072]
        xT_lo = A16[:, 3072:4096]
        x2T_hi = A16[:, 0:2048]
        x2T_lo = A16[:, 2048:4096]

        def hT_hi(r):
            return B16[:, 2048 + 512 * r:2048 + 512 * r + 256]

        def hT_lo(r):
            return B16[:, 2048 + 512 * r + 256:2048 + 512 * r + 512]

        g128 = AR[4][:, :]
        be128 = AR[6][:, :]

        def c_s1(y): return sm[:, 0 + y:1 + y]
        def c_s2(y): return sm[:, 2 + y:3 + y]
        def c_mu(y): return sm[:, 4 + y:5 + y]
        def c_msq(y): return sm[:, 6 + y:7 + y]
        def c_var(y): return sm[:, 8 + y:9 + y]
        def c_rs2(y): return sm[:, 10 + y:11 + y]
        def c_bi2(y): return sm[:, 12 + y:13 + y]
        def c_max(y): return sm[:, 16 + 16 * y:32 + 16 * y]
        def c_sum(y): return sm[:, 48 + 16 * y:64 + 16 * y]
        def c_rcp(y): return sm[:, 80 + 16 * y:96 + 16 * y]
        GT0 = 112
        c_gred = sm[:, 144:145]
        c_gs = sm[0:1, 145:146]
        c_c1 = sm[0:1, 146:147]
        c128 = sm[:, 147:148]
        negc = sm[:, 148:149]
        bcolw = sm[:, 112:120]
        epsc = sm[:, 151:152]

        pt_state = [0]

        def trans_block(src_ap, ident, waits=()):
            b = pt_state[0] % 2
            pt_state[0] += 1
            ev = sd.op("tensor", "transpose", out=PT[b][:, 0:128], in_=src_ap,
                       identity=ident, waits=[need(("pt", b))] + list(waits),
                       inc="pe")
            return b, ev

        def evac_copy(b, ev_tr, dst_ap):
            e = sd.op("vector", "tensor_copy", out=dst_ap, in_=PT[b][:, 0:128],
                      waits=[ev_tr], inc="v")
            protect(("pt", b), e)
            return e

        def evac_split(b, ev_tr, dst_hi, dst_lo):
            sd.op("vector", "tensor_copy", out=dst_hi, in_=PT[b][:, 0:128],
                  waits=[ev_tr], inc="v")
            e = sd.op("vector", "tensor_tensor", out=dst_lo, in0=PT[b][:, 0:128],
                      in1=dst_hi, op=ALU.subtract, inc="v")
            protect(("pt", b), e)
            return e

        def trans_quad(srcs, ident, waits=()):
            b = pt_state[0] % 2
            pt_state[0] += 1
            ev = None
            for k, src in enumerate(srcs):
                w = [need(("pt", b))] + list(waits) if k == 0 else []
                ev = sd.op("tensor", "transpose", out=PT[b][:, k * 128:(k + 1) * 128],
                           in_=src, identity=ident, waits=w, inc="pe")
            return b, ev, len(srcs)

        def evac_quad_split(b, ev_tr, n, dst_hi, dst_lo):
            eh = sd.op("scalar", "copy", out=dst_hi, in_=PT[b][:, 0:n * 128],
                       waits=[ev_tr], inc="s")
            e = sd.op("vector", "tensor_tensor", out=dst_lo,
                      in0=PT[b][:, 0:n * 128], in1=dst_hi, op=ALU.subtract,
                      waits=[eh], inc="v")
            protect(("pt", b), e)
            return e

        def evac_quad_copy(b, ev_tr, n, dst_ap, eng="vector"):
            meth = "tensor_copy" if eng == "vector" else "copy"
            kw = dict(out=dst_ap, in_=PT[b][:, 0:n * 128])
            e = sd.op(eng, meth, waits=[ev_tr], inc="v" if eng == "vector" else "s", **kw)
            protect(("pt", b), e)
            return e

        ev_id = dma("sync", "lc", id32[:], ident_d, serial=True)
        sd.op("vector", "memset", ones16[:], 1.0)
        sd.op("vector", "memset", A_sb[:], 0.0)
        ev_eps = sd.op("vector", "memset", epsc, EPS, inc="v")

        ev_ga = dma("sync", "lc", AR[8][0:1, :], w_d["gamma"].unsqueeze(0),
                    serial=True)
        ev_be = dma("sync", "lc", AR[5][0:1, :], w_d["beta"].unsqueeze(0),
                    serial=True)
        ev_ones = sd.op("vector", "memset", ones32[:], 1.0, inc="v")
        bc_evs = []
        for srcrow, dst in ((AR[8][0:1, :], g128), (AR[5][0:1, :], be128)):
            for nh in range(2):
                evm = sd.op("tensor", "matmul", out=P_K[:, nh * 512:(nh + 1) * 512],
                            lhsT=ones32[0:1, 0:128],
                            rhs=srcrow[0:1, nh * 512:(nh + 1) * 512],
                            start=True, stop=True, skip_group_check=True,
                            waits=[ev_ones, ev_ga, ev_be, need("P_K")], inc="pe")
            evc = sd.op("vector", "tensor_copy", out=dst, in_=P_K[:, :],
                        waits=[evm], inc="v")
            protect("P_K", evc)
            bc_evs.append(evc)
        ev_g128, ev_b128 = bc_evs
        protect(("ar", 8), ev_g128)
        protect(("ar", 3), None)
        protect(("ar", 5), ev_b128)

        WSLOT = {"Wk": 0, "Wv": 2, "Wq": 4, "Wh": 6}
        RPART = {"Wq": 0, "Wk": 32, "Wv": 64, "Wh": 96}
        BIAS = {"Wq": "bq", "Wk": "bk", "Wv": "bv", "Wh": "bh"}

        last_gamma_use = None
        last_beta_use = None
        ev_wready = {}
        for wn in ("Wk", "Wv", "Wq", "Wh"):
            slot = WSLOT[wn]
            scaled = wn != "Wh"
            last = None
            last_bcol = None
            for oc in range(DC):
                wslot = 8 if oc % 2 == 0 else 3
                wb = AR[wslot]
                ev_l = dma("sync", f"lw{oc % 2}", wb[:, :],
                           w_d[wn][oc * P:(oc + 1) * P, :],
                           waits=[need(("ar", wslot))])
                if scaled:
                    evp = sd.op("vector", "tensor_tensor", out=AR[5][:, :],
                                in0=wb[:, :], in1=be128, op=ALU.mult,
                                waits=[ev_l, ev_b128], inc="v")
                    last_beta_use = evp
                    last_bcol = sd.op("vector", "reduce_sum",
                                      out=bcolw[:, oc:oc + 1], in_=AR[5][:, :],
                                      axis=AX.X, waits=[evp], inc="v")
                    ev_l = sd.op("vector", "tensor_tensor", out=wb[:, :],
                                 in0=wb[:, :], in1=g128, op=ALU.mult,
                                 waits=[ev_g128], inc="v")
                    last_gamma_use = ev_l
                for q in range(2):
                    srcs = [wb[:, (4 * q + k) * 128:(4 * q + k + 1) * 128]
                            for k in range(4)]
                    b, evt, n = trans_quad(srcs, id32[:], waits=[ev_l, ev_id])
                    awv = AW[:, :].rearrange("p (a b) -> p a b", b=1024)
                    hs = slot * 8 + 4 * q
                    ls = (slot + 1) * 8 + 4 * q
                    dh = awv[:, hs:hs + 4, oc * 128:oc * 128 + 128]
                    dl = awv[:, ls:ls + 4, oc * 128:oc * 128 + 128]
                    ps = PT[b][:, 0:512].rearrange("p (a b) -> p a b", a=4)
                    eh = sd.op("scalar", "copy", out=dh, in_=ps,
                               waits=[evt], inc="s")
                    last = sd.op("vector", "tensor_tensor", out=dl, in0=ps,
                                 in1=dh, op=ALU.subtract, waits=[eh], inc="v")
                    protect(("pt", b), last)
                protect(("ar", wslot), last)
            if scaled:
                bk = pt_state[0] % 2
                pt_state[0] += 1
                evt = sd.op("tensor", "transpose", out=PT[bk][0:8, 0:128],
                            in_=bcolw[:, 0:8], identity=id32[:],
                            waits=[need(("pt", bk)), ev_id, last_bcol], inc="pe")
                e1 = sd.op("vector", "tensor_copy", out=AR[0][0:8, 0:128],
                           in_=PT[bk][0:8, 0:128],
                           waits=[evt, need(("ar", 0))], inc="v")
                protect(("pt", bk), e1)
                e2 = dma("gpsimd", "sr", row_d.rearrange("(a b) -> a b", a=8),
                         AR[0][0:8, 0:128], waits=[e1, need("row_d")],
                         serial=True)
                protect(("ar", 0), e2)
                e3 = dma("gpsimd", "sr", AR[1][0:1, 0:1024],
                         row_d.unsqueeze(0), waits=[e2, need(("ar", 1))],
                         serial=True)
                protect("row_d", e3)
                ev_bl = dma("sync", "lc", AR[2][0:1, :],
                            w_d[BIAS[wn]].unsqueeze(0),
                            waits=[need(("ar", 2))], serial=True)
                ev_row = sd.op("vector", "tensor_tensor", out=AR[1][0:1, :],
                               in0=AR[1][0:1, :], in1=AR[2][0:1, :], op=ALU.add,
                               waits=[e3, ev_bl], inc="v")
                rowsrc = AR[1][0:1, :]
            else:
                ev_row = dma("sync", "lc", AR[2][0:1, :],
                             w_d[BIAS[wn]].unsqueeze(0),
                             waits=[need(("ar", 2))], serial=True)
                rowsrc = AR[2][0:1, :]
            eh = sd.op("vector", "tensor_copy", out=A16[0:1, 0:1024],
                       in_=rowsrc, waits=[ev_row, need("a16lo")], inc="v")
            el = sd.op("vector", "tensor_tensor", out=A16[0:1, 1024:2048],
                       in0=rowsrc, in1=A16[0:1, 0:1024], op=ALU.subtract,
                       inc="v")
            protect(("ar", 1), el)
            protect(("ar", 2), el)
            pw = RPART[wn]
            g1 = dma("gpsimd", "sr", biasrows[pw:pw + 1, 0:1024],
                     A16[0:1, 0:1024], waits=[eh], serial=True)
            g2 = dma("gpsimd", "sr", biasrows[pw:pw + 1, 1024:2048],
                     A16[0:1, 1024:2048], waits=[el], serial=True)
            protect("a16lo", g2)
            ev_wready[wn] = [last, g1, g2]

        protect(("ar4", 0), last_gamma_use)
        protect(("ar4", 1), last_beta_use)
        protect(("ar", 6), last_beta_use)
        protect(("ar", 7), last_beta_use)

        ev_b1c = dma("sync", "lc", b1c[:, :],
                     w_d["b1"].rearrange("(c p) -> p c", p=P), serial=True,
                     allow_slow_non_contiguous=True)
        ev_b12 = [ev_b1c]
        src = w_d["b2"].unsqueeze(0)
        ev_bl = dma("sync", "lc", AR[2][0:1, :], src,
                    waits=[need(("ar", 2))], serial=True)
        eh = sd.op("vector", "tensor_copy", out=A16[0:1, 0:1024],
                   in_=AR[2][0:1, :], waits=[ev_bl, need("a16lo")], inc="v")
        el = sd.op("vector", "tensor_tensor", out=A16[0:1, 1024:2048],
                   in0=AR[2][0:1, :], in1=A16[0:1, 0:1024], op=ALU.subtract,
                   inc="v")
        protect(("ar", 2), el)
        g1 = dma("gpsimd", "sr", biasrows[96:97, 2048:3072],
                 A16[0:1, 0:1024], waits=[eh], serial=True)
        g2 = dma("gpsimd", "sr", biasrows[96:97, 3072:4096],
                 A16[0:1, 1024:2048], waits=[el], serial=True)
        protect("a16lo", g2)
        ev_b12 += [g1, g2]

        def wsl(slot, kc, lo, n):
            base = slot * 8192 + kc * 1024 + lo
            return AW[:, base:base + n]

        def proj(dst_psum, lhs_hi, lhs_lo, wn, waits):
            slot = WSLOT[wn]
            pw = RPART[wn]
            last = None
            for nh in range(2):
                first = True
                for part, lhs in ((0, lhs_hi), (1, lhs_lo), (2, lhs_hi)):
                    wslot = slot + (1 if part == 2 else 0)
                    for kc in range(DC):
                        last = sd.op("tensor", "matmul",
                                     out=dst_psum[:, nh * 512:(nh + 1) * 512],
                                     lhsT=lhs[:, kc * 128:(kc + 1) * 128],
                                     rhs=wsl(wslot, kc, nh * 512, 512),
                                     start=first, stop=False,
                                     skip_group_check=True,
                                     waits=(waits if first else []), inc="pe")
                        first = False
                sd.op("tensor", "matmul", out=dst_psum[:, nh * 512:(nh + 1) * 512],
                      lhsT=ones16[pw:pw + 1, 0:128],
                      rhs=biasrows[pw:pw + 1, nh * 512:nh * 512 + 512],
                      start=False, stop=False, tile_position=(pw, 0),
                      skip_group_check=True, inc="pe")
                last = sd.op("tensor", "matmul",
                             out=dst_psum[:, nh * 512:(nh + 1) * 512],
                             lhsT=ones16[pw:pw + 1, 0:128],
                             rhs=biasrows[pw:pw + 1, 1024 + nh * 512:1536 + nh * 512],
                             start=False, stop=True, tile_position=(pw, 0),
                             skip_group_check=True, inc="pe")
            return last

        def softmax(psum, y, ebuf_ap, tag_waits):
            p3 = psum[:, :].rearrange("p (h w) -> p h w", h=H)
            e3 = ebuf_ap.rearrange("p (h w) -> p h w", h=H)
            ev_m = sd.op("vector", "reduce_max", out=c_max(y), in_=p3, axis=AX.X,
                         waits=tag_waits, inc="v")
            ev_sub = sd.op("vector", "tensor_tensor", out=e3, in0=p3,
                           in1=c_max(y).unsqueeze(2).broadcast_to([P, H, HD]),
                           op=ALU.subtract,
                           waits=[ev_m, need(("ar", 6 + y))], inc="v")
            ev_exp = sd.op("scalar", "activation", out=ebuf_ap, in_=ebuf_ap,
                           func=AF.Exp, waits=[ev_sub], inc="s")
            ev_sm = sd.op("vector", "reduce_sum", out=c_sum(y), in_=e3, axis=AX.X,
                          waits=[ev_exp], inc="v")
            ev_r = sd.op("vector", "reciprocal", out=c_rcp(y), in_=c_sum(y),
                         waits=[ev_sm], inc="v")
            return ev_exp, ev_sub, ev_r

        def ln_finalize(y, ev_1, ev_2, dst_sc, dst_bi):
            ev_mu = sd.op("vector", "tensor_scalar", out=c_mu(y), in0=c_s1(y),
                          scalar1=1.0 / D, scalar2=None, op0=ALU.mult,
                          waits=[ev_1], inc="v")
            ev_ms = sd.op("vector", "tensor_tensor", out=c_msq(y), in0=c_mu(y),
                          in1=c_mu(y), op=ALU.mult, inc="v")
            ev_va = sd.op("vector", "scalar_tensor_tensor", out=c_var(y),
                          in0=c_s2(y), scalar=1.0 / D, in1=c_msq(y),
                          op0=ALU.mult, op1=ALU.subtract, waits=[ev_2, ev_ms],
                          inc="v")
            ev_sd = sd.op("scalar", "activation", out=c_msq(y), in_=c_var(y),
                          func=AF.Sqrt, bias=epsc, scale=1.0, waits=[ev_va],
                          inc="s")
            ev_rs = sd.op("vector", "reciprocal", out=dst_sc, in_=c_msq(y),
                          waits=[ev_sd], inc="v")
            ev_bi = sd.op("vector", "scalar_tensor_tensor", out=dst_bi,
                          in0=c_mu(y), scalar=-1.0, in1=dst_sc, op0=ALU.mult,
                          op1=ALU.mult, waits=[ev_rs], inc="v")
            return ev_bi

        def ln_stats(y, src_ap, tmp_ap, dst_sc, dst_bi, waits):
            gw = waits + [need(("ar4", y)), need(("s1", y))]
            ev_1 = sd.op("scalar", "activation", out=tmp_ap, in_=src_ap,
                         func=AF.Identity, accum_out=c_s1(y), waits=gw,
                         inc="s")
            ev_2 = sd.op("scalar", "activation", out=tmp_ap, in_=src_ap,
                         func=AF.Square, accum_out=c_s2(y), waits=[ev_1],
                         inc="s")
            ev_bi = ln_finalize(y, ev_1, ev_2, dst_sc, dst_bi)
            protect(("s1", y), ev_bi)
            protect(("ar4", y), ev_2)
            return ev_bi, ev_2

        evs = {}

        gacc = {"W1": sm[:, GT0 + 0:GT0 + 1], "W2": sm[:, GT0 + 1:GT0 + 2]}
        gtmp2 = sm[:, GT0 + 2:GT0 + 3]
        ev_gz = sd.op("vector", "memset", sm[:, GT0:GT0 + 2], 0.0, inc="v")

        def gamma_unit(wname, tile_ap):
            ev_l = dma("gpsimd", "lg", AR[8][:, :], tile_ap,
                       waits=[need(("ar", 8))])
            ev_r = sd.op("vector", "tensor_reduce", out=gtmp2, in_=AR[8][:, :],
                         axis=AX.X, op=ALU.add, apply_absolute_value=True,
                         waits=[ev_l, need("gtmp2")], inc="v")
            protect(("ar", 8), ev_r)
            ev_a = sd.op("vector", "tensor_tensor", out=gacc[wname],
                         in0=gacc[wname], in1=gtmp2, op=ALU.add,
                         waits=[ev_r, ev_gz], inc="v")
            protect("gtmp2", ev_a)
            return ev_a

        prep_gammas = ([("gW1", t) for t in range(FCN)] +
                       [("gW2", t) for t in range(FCN)])

        def run_gamma(u):
            kind, t = u
            if kind == "gW1":
                ev = gamma_unit("W1", w_d["W1"][t * P:(t + 1) * P, :])
                if t == FCN - 1:
                    protect("gacc_W1", ev)
            else:
                ev = gamma_unit("W2", w_d["W2"][(t // 4) * P:(t // 4 + 1) * P,
                                                (t % 4) * 1024:(t % 4 + 1) * 1024])
                if t == FCN - 1:
                    protect("gacc_W2", ev)

        def prep_gamma_unit(j):
            per = max(1, (2 * FCN + NT - 1) // NT)
            for _ in range(per):
                if prep_gammas:
                    run_gamma(prep_gammas.pop(0))

        sd.op("vector", "engine_nop", waits=[("sr", sd.cnt["sr"]),
                                             ("lc", sd.cnt["lc"])])
        def emitA1(i):
            y = i % 2
            b, r = i // TPB, (i % TPB) * P
            ev_x = dma("sync", f"lx{y}", AR[y][:, :], x_d[b, r:r + P, :],
                       waits=[need(("ar", y))])
            evs["x", i] = ev_x
            ev_bi, _ = ln_stats(y, AR[y][:, :], AR[4 + y][:, :],
                                st_sc[:, i:i + 1], st_bi[:, i:i + 1], [ev_x])
            ev_nx = sd.op("vector", "tensor_scalar", out=AR[2 + y][:, :],
                          in0=AR[y][:, :], scalar1=st_sc[:, i:i + 1],
                          scalar2=st_bi[:, i:i + 1], op0=ALU.mult, op1=ALU.add,
                          waits=[ev_bi, need(("ar", 2 + y))], inc="v")
            evs["nx", i] = ev_nx
            protect(("ar", y), ev_nx)

        def trans_nx1_pe(i):
            y = i % 2
            quads = []
            for q in range(2):
                srcs = [AR[2 + y][:, (4 * q + k) * 128:(4 * q + k + 1) * 128]
                        for k in range(4)]
                b, evt, n = trans_quad(srcs, id32[:], waits=[evs["nx", i]])
                quads.append((b, evt, n, q))
            return quads

        def trans_nx1_evac(i, quads):
            last = None
            for b, evt, n, q in quads:
                last = evac_quad_split(b, evt, n,
                                       nxT_hi[:, q * 512:(q + 1) * 512],
                                       nxT_lo[:, q * 512:(q + 1) * 512])
            evs["nxT", i] = last

        def trans_nx1(i):
            trans_nx1_evac(i, trans_nx1_pe(i))

        def emitB1(j):
            y = j % 2
            b, tib = j // TPB, j % TPB
            ev_k = proj(P_K, nxT_hi, nxT_lo, "Wk",
                        [evs["nxT", j]] + ev_wready["Wk"] + [need("P_K")])
            ev_v = proj(P_V, nxT_hi, nxT_lo, "Wv",
                        ev_wready["Wv"] + [need("P_V")])
            quads = trans_nx1_pe(j + 1) if j + 1 < NT else None
            ev_exp, ev_sub, ev_r = softmax(P_K, y, AR[6 + y][:, :], [ev_k])
            protect("P_K", ev_sub)
            ev_vp = sd.op("vector", "tensor_tensor",
                          out=AR[2 + y][:, :].rearrange("p (h w) -> p h w", h=H),
                          in0=P_V[:, :].rearrange("p (h w) -> p h w", h=H),
                          in1=c_rcp(y).unsqueeze(2).broadcast_to([P, H, HD]),
                          op=ALU.mult, waits=[ev_v, ev_r], inc="v")
            protect("P_V", ev_vp)
            ek16 = B16[:, 0:2048]
            v16 = B16[:, 2048:4096]
            ev_eh = sd.op("scalar", "copy", out=ek16[:, 0:1024],
                          in_=AR[6 + y][:, :],
                          waits=[ev_exp, need(("b16", 0))], inc="s")
            ev_el = sd.op("vector", "tensor_tensor", out=ek16[:, 1024:2048],
                          in0=AR[6 + y][:, :], in1=ek16[:, 0:1024],
                          op=ALU.subtract, waits=[ev_eh], inc="v")
            ev_vh = sd.op("scalar", "copy", out=v16[:, 0:1024],
                          in_=AR[2 + y][:, :],
                          waits=[ev_vp, need(("b16", 1))], inc="s")
            ev_vl = sd.op("vector", "tensor_tensor", out=v16[:, 1024:2048],
                          in0=AR[2 + y][:, :], in1=v16[:, 0:1024],
                          op=ALU.subtract, waits=[ev_vh], inc="v")
            if quads is not None:
                trans_nx1_evac(j + 1, quads)
            ev_A = None
            for h in range(H):
                hp = (h % 2) * 64
                for pi, (lo, ro) in enumerate(((0, 0), (1024, 0), (0, 1024))):
                    w = [ev_el, ev_vl]
                    if tib == 0 and h == 0 and pi == 0:
                        w += [need("P_A")]
                    ev_A = sd.op("tensor", "matmul",
                                 out=P_A[hp:hp + 64,
                                         (h // 2) * 64:(h // 2) * 64 + 64],
                                 lhsT=ek16[:, lo + h * 64:lo + (h + 1) * 64],
                                 rhs=v16[:, ro + h * 64:ro + (h + 1) * 64],
                                 start=(tib == 0 and h <= 1 and pi == 0),
                                 stop=(tib == TPB - 1 and h == H - 1 and pi == 2),
                                 tile_position=(0, hp), skip_group_check=True,
                                 waits=w, inc="pe")
            protect(("ar", 6 + y), ev_A)
            protect(("ar", 2 + y), ev_A)
            protect(("b16", 0), ev_A)
            protect(("b16", 1), ev_A)
            if tib == TPB - 1:
                e = None
                for h in range(H):
                    hp = (h % 2) * 64
                    c = h // 2
                    src = P_A[hp:hp + 64, c * 64:c * 64 + 64]
                    dhi = A_sb[hp:hp + 64,
                               b * 2048 + c * 128 + hp:b * 2048 + c * 128 + hp + 64]
                    dlo = A_sb[hp:hp + 64,
                               b * 2048 + 1024 + c * 128 + hp:
                               b * 2048 + 1024 + c * 128 + hp + 64]
                    eh = sd.op("vector", "tensor_copy", out=dhi, in_=src,
                               waits=[ev_A], inc="v")
                    e = sd.op("vector", "tensor_tensor", out=dlo, in0=src,
                              in1=dhi, op=ALU.subtract, inc="v")
                protect("P_A", e)
                evs["Asb", b] = e

        emitA1(0)
        trans_nx1(0)
        for j in range(NT):
            if j + 1 < NT:
                emitA1(j + 1)
            emitB1(j)
            prep_gamma_unit(j)

        def emitA2(i):
            y = i % 2
            b, r = i // TPB, (i % TPB) * P
            ev_x = dma("sync", f"lx{y}", AR[y][:, :], x_d[b, r:r + P, :],
                       waits=[need(("ar", y))])
            evs["x", i] = ev_x
            ev_nx = sd.op("vector", "tensor_scalar", out=AR[2 + y][:, :],
                          in0=AR[y][:, :], scalar1=st_sc[:, i:i + 1],
                          scalar2=st_bi[:, i:i + 1], op0=ALU.mult, op1=ALU.add,
                          waits=[ev_x, need(("ar", 2 + y))], inc="v")
            evs["nx", i] = ev_nx

        def trans_pair2(i):
            y = i % 2
            last = None
            for src, dhi, dlo, sev in ((AR[2 + y], nxT_hi, nxT_lo, evs["nx", i]),
                                       (AR[y], xT_hi, xT_lo, evs["x", i])):
                for q in range(2):
                    srcs = [src[:, (4 * q + k) * 128:(4 * q + k + 1) * 128]
                            for k in range(4)]
                    b, evt, n = trans_quad(srcs, id32[:], waits=[sev])
                    last = evac_quad_split(b, evt, n,
                                           dhi[:, q * 512:(q + 1) * 512],
                                           dlo[:, q * 512:(q + 1) * 512])
            evs["xT", i] = last
            protect(("ar", 2 + y), last)

        def emitB2(j):
            y = j % 2
            b = j // TPB
            ev_q = proj(P_K, nxT_hi, nxT_lo, "Wq",
                        [evs["xT", j]] + ev_wready["Wq"] + [need("P_K")])
            ev_h = proj(P_V, xT_hi, xT_lo, "Wh",
                        ev_wready["Wh"] + [need("P_V")])
            ev_exp, ev_sub, ev_r = softmax(P_K, y, AR[6 + y][:, :], [ev_q])
            protect("P_K", ev_sub)
            ev_qsm = sd.op("vector", "tensor_tensor",
                           out=AR[4 + y][:, :].rearrange("p (h w) -> p h w", h=H),
                           in0=AR[6 + y][:, :].rearrange("p (h w) -> p h w", h=H),
                           in1=c_rcp(y).unsqueeze(2).broadcast_to([P, H, HD]),
                           op=ALU.mult,
                           waits=[ev_exp, ev_r, need(("ar4", y))], inc="v")
            protect(("ar", 6 + y), ev_qsm)
            q16 = B16[:, 0:2048]
            last_ev = None
            evt = None
            for q in range(2):
                srcs = [AR[4 + y][:, (4 * q + k) * 128:(4 * q + k + 1) * 128]
                        for k in range(4)]
                bk, evt, n = trans_quad(srcs, id32[:],
                                        waits=[ev_qsm] + ([need(("b16", 0))] if q == 0 else []))
                last_ev = evac_quad_split(bk, evt, n,
                                          q16[:, q * 512:(q + 1) * 512],
                                          q16[:, 1024 + q * 512:1024 + (q + 1) * 512])
            if j + 1 < NT:
                trans_pair2(j + 1)
            ev_y = None
            for c in range(DC):
                for lo, ro in ((0, 0), (1024, 0), (0, 1024)):
                    ev_y = sd.op("tensor", "matmul",
                                 out=P_V[:, c * 128:(c + 1) * 128],
                                 lhsT=q16[:, lo + c * 128:lo + c * 128 + 128],
                                 rhs=A_sb[:, b * 2048 + ro + c * 128:
                                          b * 2048 + ro + c * 128 + 128],
                                 start=False, stop=(c == DC - 1 and ro == 1024),
                                 skip_group_check=True,
                                 waits=[last_ev, ev_h, evs["Asb", b]], inc="pe")
            protect(("b16", 0), ev_y)
            ev_r1 = sd.op("vector", "scalar_tensor_tensor", out=AR[y][:, :],
                          in0=AR[y][:, :], scalar=1.0, in1=P_V[:, :],
                          op0=ALU.mult, op1=ALU.add, accum_out=c_s1(y),
                          waits=[ev_y, need(("s1", y))], inc="v")
            protect("P_V", ev_r1)
            ev_2 = sd.op("scalar", "activation", out=AR[4 + y][:, :],
                         in_=AR[y][:, :], func=AF.Square, accum_out=c_s2(y),
                         waits=[ev_r1, evt, need(("s1", y))], inc="s")
            ev_bi = ln_finalize(y, ev_r1, ev_2, c_rs2(y), c_bi2(y))
            protect(("s1", y), ev_bi)
            ev_x2 = sd.op("scalar", "activation", out=AR[4 + y][:, :],
                          in_=AR[y][:, :], func=AF.Identity, scale=c_rs2(y),
                          bias=c_bi2(y), waits=[ev_bi, ev_2], inc="s")
            protect(("ar", y), ev_x2)
            bb, r = j // TPB, (j % TPB) * P
            ev_st = dma("gpsimd", f"sx{y}", x2_d[bb, r:r + P, :],
                        AR[4 + y][:, :], waits=[ev_x2])
            evs["x2st", j] = ev_st
            protect(("ar4", y), ev_st)

        emitA2(0)
        trans_pair2(0)
        for j in range(NT):
            if j + 1 < NT:
                emitA2(j + 1)
            emitB2(j)

        evP2 = ("pe", sd.cnt["pe"])
        protect(("ar", 8), [evP2])
        protect(("ar", 3), [evP2])
        protect("a16lo", [evP2])
        protect("a16wq", [evP2])
        protect(("PO", 0), (need("P_K") or []))
        protect(("PO", 1), (need("P_V") or []))

        def ffn_gamma(wname):
            while prep_gammas:
                run_gamma(prep_gammas.pop(0))
            ev_mm = sd.op("tensor", "transpose", out=P_T0[0:1, 0:128],
                          in_=gacc[wname], identity=id32[:],
                          waits=[need("gacc_" + wname), need(("pt", 0)), ev_id], inc="pe")
            ev_cp = sd.op("vector", "reduce_sum", out=c_gs,
                          in_=P_T0[0:1, 0:128], axis=AX.X,
                          waits=[ev_mm], inc="v")
            protect(("pt", 0), ev_cp)
            ev_c1 = sd.op("vector", "tensor_scalar", out=c_c1, in0=c_gs,
                          scalar1=0.5 / (FF * D), scalar2=0.5 * EPS,
                          op0=ALU.mult, op1=ALU.add, waits=[ev_cp], inc="v")
            ev_h16 = sd.op("vector", "tensor_copy", out=cb16[0:1, 0:1],
                           in_=c_c1, waits=[ev_c1], inc="v")
            ev_d1 = sd.op("vector", "tensor_tensor", out=sm[0:1, 150:151],
                          in0=c_c1, in1=cb16[0:1, 0:1], op=ALU.subtract,
                          waits=[ev_h16], inc="v")
            ev_l16 = sd.op("vector", "tensor_copy", out=cb16[0:1, 1:2],
                           in_=sm[0:1, 150:151], waits=[ev_d1], inc="v")
            ev_ll = sd.op("vector", "tensor_tensor", out=cb16[0:1, 2:3],
                          in0=sm[0:1, 150:151], in1=cb16[0:1, 1:2],
                          op=ALU.subtract, waits=[ev_l16], inc="v")
            sd.op("tensor", "matmul", out=P_SP[:, 0:1],
                  lhsT=ones16[0:1, 0:128], rhs=cb16[0:1, 0:1], start=True,
                  stop=False, skip_group_check=True,
                  waits=[ev_ll, need("P_SPbc")], inc="pe")
            sd.op("tensor", "matmul", out=P_SP[:, 0:1],
                  lhsT=ones16[0:1, 0:128], rhs=cb16[0:1, 1:2],
                  start=False, stop=False, skip_group_check=True, inc="pe")
            ev_bm = sd.op("tensor", "matmul", out=P_SP[:, 0:1],
                          lhsT=ones16[0:1, 0:128], rhs=cb16[0:1, 2:3],
                          start=False, stop=True, skip_group_check=True,
                          inc="pe")
            ev_bc = sd.op("vector", "tensor_copy", out=c128, in_=P_SP[:, 0:1],
                          waits=[ev_bm], inc="v")
            protect("P_SPbc", ev_bc)
            ev_ng = sd.op("vector", "tensor_scalar", out=negc, in0=c128,
                          scalar1=-1.0, scalar2=None, op0=ALU.mult,
                          waits=[ev_bc], inc="v")
            return [ev_bc, ev_ng]

        def ffn_quant(ntiles, tile_ap, dst_quad, ev_c):
            last = None
            for t in range(ntiles):
                wslot = 8 if t % 2 == 0 else 3
                wb = AR[wslot]
                ev_l = dma("sync", f"lw{t % 2}", wb[:, :], tile_ap(t),
                           waits=[need(("ar", wslot))])
                ev_pa = sd.op("vector", "tensor_scalar", out=AR[5][:, :],
                              in0=wb[:, :], scalar1=c128[:, 0:1], scalar2=None,
                              op0=ALU.is_gt,
                              waits=[ev_l, need(("ar4", 1)), need(("ar", 5))] + ev_c,
                              inc="v")
                ev_pb = sd.op("vector", "tensor_scalar", out=AR[6][:, :],
                              in0=wb[:, :], scalar1=negc[:, 0:1], scalar2=None,
                              op0=ALU.is_lt, waits=[need(("ar", 6))], inc="v")
                ev_wq = sd.op("vector", "tensor_tensor", out=AR[7][:, :],
                              in0=AR[5][:, :], in1=AR[6][:, :],
                              op=ALU.subtract,
                              waits=[ev_pb, need(("ar", 7))], inc="v")
                protect(("ar", wslot), ev_wq)
                protect(("ar4", 1), ev_wq)
                protect(("ar", 5), ev_wq)
                protect(("ar", 6), ev_wq)
                for q in range(2):
                    srcs = [AR[7][:, (4 * q + k) * 128:(4 * q + k + 1) * 128]
                            for k in range(4)]
                    bk, evt, n = trans_quad(srcs, id32[:], waits=[ev_wq])
                    last = sd.op("vector", "tensor_copy",
                                 out=dst_quad(t, q),
                                 in_=PT[bk][:, 0:512].rearrange("p (a b) -> p a b", a=4),
                                 waits=[evt], inc="v")
                    protect(("pt", bk), last)
                protect(("ar", 7), last)
            return last

        ev_c1l = ffn_gamma("W1")
        def w1_dst_quad(t, q):
            awv = AW[:, :].rearrange("p (a b) -> p a b", b=4096)
            return awv[:, 4 * q:4 * q + 4, t * 128:t * 128 + 128]

        ev_w1 = ffn_quant(FCN, lambda t: w_d["W1"][t * P:(t + 1) * P, :],
                          w1_dst_quad, ev_c1l)
        ev_c2l = ffn_gamma("W2")
        def w2_dst_quad(t, q):
            awv = AW[:, :].rearrange("p (a b) -> p a b", b=1024)
            si = 32 + (t % 4) * 8 + 4 * q
            return awv[:, si:si + 4, (t // 4) * 128:(t // 4) * 128 + 128]

        ev_w2 = ffn_quant(FCN,
                          lambda t: w_d["W2"][(t // 4) * P:(t // 4 + 1) * P,
                                              (t % 4) * 1024:(t % 4 + 1) * 1024],
                          w2_dst_quad, ev_c2l)
        protect(("hT32", 0), (need(("ar", 8)) or []))
        protect(("hT32", 1), (need(("ar", 8)) or []))
        protect(("hT16", 0), (need(("b16", 1)) or []) + (need(("b16", 0)) or []))
        protect(("hT16", 1), (need(("b16", 1)) or []) + (need(("b16", 0)) or []))

        def emitA3(s):
            p = s % 2
            for u in range(2):
                j = 2 * s + u
                b, r = j // TPB, (j % TPB) * P
                ev = dma("sync", f"l2{2 * p + u}", AR[2 * p + u][:, :],
                         x2_d[b, r:r + P, :],
                         waits=[evs["x2st", j], need(("ar", 2 * p + u))])
                evs["x2ld", j] = ev

        def trans3(s):
            p = s % 2
            xa, xb = AR[2 * p], AR[2 * p + 1]
            last = None
            for q in range(4):
                srcs = []
                for c in (2 * q, 2 * q + 1):
                    for src in (xa, xb):
                        srcs.append(src[:, c * 128:(c + 1) * 128])
                bb_ = pt_state[0] % 2
                bk, evt, n = trans_quad(srcs, id32[:],
                                        waits=[evs["x2ld", 2 * s],
                                               evs["x2ld", 2 * s + 1],
                                               need(("g", 2 + bb_))])
                last = evac_quad_split(bk, evt, n,
                                       x2T_hi[:, q * 512:(q + 1) * 512],
                                       x2T_lo[:, q * 512:(q + 1) * 512])
                protect(("g", 2 + bk), last)
            evs["x2T", s] = last

        def emit_g(s, fc):
            r = fc % 4
            gp = GSL[r][:, 0:256]
            first = True
            ev = None
            for kc in range(DC):
                for x2t in (x2T_hi, x2T_lo):
                    w = []
                    if first:
                        w = [evs["x2T", s], ev_w1, ev_w2,
                             need(("g", r))] + list(ev_b12)
                    ev = sd.op("tensor", "matmul", out=gp,
                               lhsT=AW[:, kc * 4096 + fc * 128:
                                       kc * 4096 + fc * 128 + 128],
                               rhs=x2t[:, kc * 256:(kc + 1) * 256],
                               start=first,
                               stop=(kc == DC - 1 and x2t is x2T_lo),
                               skip_group_check=True, waits=w, inc="pe")
                    first = False
            evs["g", fc] = ev

        def emit_relu_split(fc):
            r = fc % 4
            ev_rel = sd.op("scalar", "activation",
                           out=AR[8][:, 256 * r:256 * r + 256],
                           in_=GSL[r][:, 0:256], func=AF.Relu,
                           bias=b1c[:, fc:fc + 1], scale=1.0,
                           waits=[evs["g", fc], need(("hT32", r))], inc="s")
            protect(("g", r), ev_rel)
            if r >= 2:
                protect(("pt", r - 2), ev_rel)
            ev_hi = sd.op("scalar", "copy", out=hT_hi(r),
                          in_=AR[8][:, 256 * r:256 * r + 256],
                          waits=[ev_rel, need(("hT16", r))], inc="s")
            ev_lo = sd.op("vector", "tensor_tensor", out=hT_lo(r),
                          in0=AR[8][:, 256 * r:256 * r + 256], in1=hT_hi(r),
                          op=ALU.subtract, waits=[ev_hi], inc="v")
            protect(("hT32", r), ev_lo)
            evs["hT", fc] = ev_lo

        def emit_ffn2(fc):
            r = fc % 4
            ev = None
            first = True
            for sub in range(2):
                for ring in (hT_hi(r), hT_lo(r)):
                    for nh in range(2):
                        ev = sd.op("tensor", "matmul",
                                   out=P_O[sub][:, nh * 512:(nh + 1) * 512],
                                   lhsT=ring[:, sub * 128:sub * 128 + 128],
                                   rhs=AW[:, 32768 + fc * 1024 + nh * 512:
                                          32768 + fc * 1024 + nh * 512 + 512],
                                   start=False, stop=(fc == FCN - 1),
                                   skip_group_check=True,
                                   waits=[evs["hT", fc]] if first else [],
                                   inc="pe")
                        first = False
            protect(("hT16", r), ev)

        def fcloop(s):
            emit_g(s, 0)
            emit_relu_split(0)
            emit_g(s, 1)
            emit_g(s, 2)
            emit_g(s, 3)
            for sub in range(2):
                for nh in range(2):
                    sd.op("tensor", "matmul",
                          out=P_O[sub][:, nh * 512:(nh + 1) * 512],
                          lhsT=ones16[96:97, 0:128],
                          rhs=biasrows[96:97, 2048 + nh * 512:2048 + nh * 512 + 512],
                          start=True, stop=False, tile_position=(96, 0),
                          skip_group_check=True,
                          waits=[need(("PO", sub))] + list(ev_b12), inc="pe")
                    sd.op("tensor", "matmul",
                          out=P_O[sub][:, nh * 512:(nh + 1) * 512],
                          lhsT=ones16[96:97, 0:128],
                          rhs=biasrows[96:97, 3072 + nh * 512:3584 + nh * 512],
                          start=False, stop=False, tile_position=(96, 0),
                          skip_group_check=True, inc="pe")
            for fc in range(FCN):
                if fc >= 1:
                    emit_relu_split(fc)
                emit_ffn2(fc)
                if fc + 4 < FCN:
                    emit_g(s, fc + 4)
            evs["ffn2end", s] = ("pe", sd.cnt["pe"])

        def epi(s):
            p = s % 2
            xa, xb = AR[2 * p], AR[2 * p + 1]
            for sub in range(2):
                xs = (xa, xb)[sub]
                j = 2 * s + sub
                y = sub
                ev_r2 = sd.op("vector", "scalar_tensor_tensor", out=xs[:, :],
                              in0=xs[:, :], scalar=1.0, in1=P_O[sub][:, :],
                              op0=ALU.mult, op1=ALU.add, accum_out=c_s1(y),
                              waits=[evs["ffn2end", s], need(("s1", y))],
                              inc="v")
                protect(("PO", sub), ev_r2)
                ev_2 = sd.op("scalar", "activation", out=AR[4 + y][:, :],
                             in_=xs[:, :], func=AF.Square, accum_out=c_s2(y),
                             waits=[ev_r2, need(("ar4", y)),
                                    need(("s1", y))], inc="s")
                ev_bi = ln_finalize(y, ev_r2, ev_2, c_rs2(y), c_bi2(y))
                protect(("s1", y), ev_bi)
                protect(("ar4", y), ev_2)
                ev_o = sd.op("scalar", "activation", out=AR[6 + sub][:, :],
                             in_=xs[:, :], func=AF.Identity, scale=c_rs2(y),
                             bias=c_bi2(y),
                             waits=[ev_bi, need(("ar", 6 + sub))], inc="s")
                protect(("ar", 2 * p + sub), ev_o)
                b, r = j // TPB, (j % TPB) * P
                ev_os = dma("gpsimd", f"so{sub}", out_d[b, r:r + P, :],
                            AR[6 + sub][:, :], waits=[ev_o])
                protect(("ar", 6 + sub), ev_os)

        emitA3(0)
        trans3(0)
        for s3 in range(NSUP):
            fcloop(s3)
            if s3 + 1 < NSUP:
                emitA3(s3 + 1)
                trans3(s3 + 1)
            epi(s3)

        sd.op("gpsimd", "nop",
              waits=[(k, sd.cnt[k]) for k in ("sr", "sx0", "sx1", "so0", "so1")])

        with nc.Block() as block:
            @block.sync
            def _(h):
                sd.replay("sync", h, sems)

            @block.gpsimd
            def _(h):
                sd.replay("gpsimd", h, sems)

            @block.vector
            def _(h):
                sd.replay("vector", h, sems)

            @block.scalar
            def _(h):
                sd.replay("scalar", h, sems)

            @block.tensor
            def _(h):
                sd.replay("tensor", h, sems)

    return nc


def kernel(L0_attn_x, L0_attn_Wq, L0_attn_bq, L0_attn_Wk, L0_attn_bk,
           L0_attn_Wv, L0_attn_bv, L0_attn_Wh, L0_attn_bh, L0_attn_gamma,
           L0_attn_beta, L0_ffn_W1, L0_ffn_b1, L0_ffn_W2, L0_ffn_b2):
    x = np.ascontiguousarray(np.asarray(L0_attn_x, dtype=np.float32))
    NB, T, Dm = x.shape
    ncores = 8
    Bc = NB // ncores
    common = {
        "Wq": np.ascontiguousarray(L0_attn_Wq, dtype=np.float32),
        "bq": np.ascontiguousarray(L0_attn_bq, dtype=np.float32),
        "Wk": np.ascontiguousarray(L0_attn_Wk, dtype=np.float32),
        "bk": np.ascontiguousarray(L0_attn_bk, dtype=np.float32),
        "Wv": np.ascontiguousarray(L0_attn_Wv, dtype=np.float32),
        "bv": np.ascontiguousarray(L0_attn_bv, dtype=np.float32),
        "Wh": np.ascontiguousarray(L0_attn_Wh, dtype=np.float32),
        "bh": np.ascontiguousarray(L0_attn_bh, dtype=np.float32),
        "gamma": np.ascontiguousarray(L0_attn_gamma, dtype=np.float32),
        "beta": np.ascontiguousarray(L0_attn_beta, dtype=np.float32),
        "W1": np.ascontiguousarray(L0_ffn_W1, dtype=np.float32),
        "b1": np.ascontiguousarray(L0_ffn_b1, dtype=np.float32),
        "W2": np.ascontiguousarray(L0_ffn_W2, dtype=np.float32),
        "b2": np.ascontiguousarray(L0_ffn_b2, dtype=np.float32),
        "ident": np.eye(128, dtype=np.float32),
    }
    nc = build_nc(Bc, T)
    in_maps = [dict(common, x=np.ascontiguousarray(x[c * Bc:(c + 1) * Bc]))
               for c in range(ncores)]
    res = run_bass_kernel_spmd(nc, in_maps, core_ids=list(range(ncores)))
    return np.concatenate([r["out"] for r in res.results], axis=0)


# version 21
# speedup vs baseline: 1.1150x; 1.1150x over previous
import contextlib
import numpy as np

import concourse.bass as bass
import concourse.mybir as mybir
from concourse.bass_utils import run_bass_kernel_spmd

F32 = mybir.dt.float32
BF16 = mybir.dt.bfloat16
AF = mybir.ActivationFunctionType
ALU = mybir.AluOpType
AX = mybir.AxisListType

P = 128
D = 1024
DC = D // P
H = 16
HD = 64
FF = 4096
FCN = FF // P
EPS = 1e-5


class Sched:
    ENGINES = ("sync", "gpsimd", "vector", "scalar", "tensor")

    def __init__(self):
        self.q = {e: [] for e in self.ENGINES}
        import collections
        self.cnt = collections.defaultdict(int)
        self.lastw = {e: {} for e in self.ENGINES}

    def op(self, eng, meth, *args, waits=(), inc=None, amt=1, **kw):
        flat = []
        for w in waits:
            if w is None:
                continue
            if isinstance(w, list):
                flat.extend(x for x in w if x is not None)
            else:
                flat.append(w)
        fws = []
        for s, val in flat:
            if self.lastw[eng].get(s, 0) >= val:
                continue
            self.lastw[eng][s] = val
            fws.append((s, val))
        ev = None
        if inc is not None:
            self.cnt[inc] += amt
            ev = (inc, self.cnt[inc])
        self.q[eng].append((fws, meth, args, kw, inc, amt))
        return ev

    def replay(self, eng, h, sems):
        drains = eng in ("vector", "scalar")
        first = True
        for fws, meth, args, kw, inc, amt in self.q[eng]:
            if drains and not first:
                h.drain()
            first = False
            for s, v in fws:
                h.wait_ge(sems[s], v)
            ins = getattr(h, meth)(*args, **kw)
            if inc is not None:
                ins.then_inc(sems[inc], amt)


def build_nc(B, T):
    TPB = T // P
    NT = B * TPB
    NSUP = NT // 2
    assert T % 256 == 0

    nc = bass.Bass("TRN2", target_bir_lowering=False)

    x_d = nc.dram_tensor("x", [B, T, D], F32, kind="ExternalInput").ap()
    ident_d = nc.dram_tensor("ident", [P, P], F32, kind="ExternalInput").ap()
    w_d = {}
    for nm, shp in [("Wq", [D, D]), ("bq", [D]), ("Wk", [D, D]), ("bk", [D]),
                    ("Wv", [D, D]), ("bv", [D]), ("Wh", [D, D]), ("bh", [D]),
                    ("gamma", [D]), ("beta", [D]),
                    ("W1", [FF, D]), ("b1", [FF]), ("W2", [D, FF]), ("b2", [D])]:
        w_d[nm] = nc.dram_tensor(nm, shp, F32, kind="ExternalInput").ap()
    out_d = nc.dram_tensor("out", [B, T, D], F32, kind="ExternalOutput").ap()
    x2_d = nc.dram_tensor("x2scr", [B, T, D], F32, kind="Internal").ap()
    row_d = nc.dram_tensor("rowscr", [D], F32, kind="Internal").ap()

    sd = Sched()
    guards = {}

    def need(key):
        return guards.get(key)

    def protect(key, ev):
        guards[key] = ev if isinstance(ev, list) else [ev]

    def dma(eng, sem, out, in_, waits=(), serial=False, **kw):
        w = list(waits)
        if serial:
            w.append((sem, sd.cnt[sem]))
        return sd.op(eng, "dma_start", out=out, in_=in_, waits=w,
                     inc=sem, amt=16, **kw)

    with contextlib.ExitStack() as es:
        def sbuf(name, shape, dt):
            return es.enter_context(nc.sbuf_tensor(name, shape, dt))

        AW = sbuf("AW", [P, 65536], BF16)
        AR = [sbuf(f"ar{i}", [P, 1024], F32) for i in range(9)]
        A16 = sbuf("a16", [P, 4096], BF16)
        A_sb = sbuf("A_sb", [P, B * 2048], BF16)
        biasrows = sbuf("biasrows", [P, 4096], BF16)
        b1c = sbuf("b1c", [P, FCN], F32)
        B16 = sbuf("B16", [P, 4096], BF16)
        id32 = sbuf("id32", [P, P], F32)
        ones16 = sbuf("ones16", [P, 256], BF16)
        ones32 = sbuf("ones32", [P, P], F32)
        cb16 = sbuf("cb16", [1, 4], BF16)
        sm = sbuf("sm", [P, 152], F32)
        st_sc = sbuf("st_sc", [P, NT], F32)
        st_bi = sbuf("st_bi", [P, NT], F32)

        P_K = es.enter_context(nc.psum_tensor("P_K", [P, 1024], F32))
        P_V = es.enter_context(nc.psum_tensor("P_V", [P, 1024], F32))
        P_A = es.enter_context(nc.psum_tensor("P_A", [P, 512], F32))
        P_SP = es.enter_context(nc.psum_tensor("P_SP", [P, 512], F32))
        P_T0 = es.enter_context(nc.psum_tensor("P_T0", [P, 512], F32))
        P_T1 = es.enter_context(nc.psum_tensor("P_T1", [P, 512], F32))
        PT = (P_T0, P_T1)
        P_O = (P_K, P_V)
        GSL = (P_A, P_SP, P_T0, P_T1)

        sems = {}
        for s in ("lc", "lw0", "lw1", "lg", "lx0", "lx1", "l20", "l21", "l22",
                  "l23", "sx0", "sx1", "so0", "so1", "sr", "pe", "v", "s", "g"):
            sems[s] = es.enter_context(nc.semaphore(s))

        nxT_hi = A16[:, 0:1024]
        nxT_lo = A16[:, 1024:2048]
        xT_hi = A16[:, 2048:3072]
        xT_lo = A16[:, 3072:4096]
        x2T_hi = A16[:, 0:2048]
        x2T_lo = A16[:, 2048:4096]

        def hT_hi(r):
            return B16[:, 2048 + 512 * r:2048 + 512 * r + 256]

        def hT_lo(r):
            return B16[:, 2048 + 512 * r + 256:2048 + 512 * r + 512]

        g128 = AR[4][:, :]
        be128 = AR[6][:, :]

        def c_s1(y): return sm[:, 0 + y:1 + y]
        def c_s2(y): return sm[:, 2 + y:3 + y]
        def c_mu(y): return sm[:, 4 + y:5 + y]
        def c_msq(y): return sm[:, 6 + y:7 + y]
        def c_var(y): return sm[:, 8 + y:9 + y]
        def c_rs2(y): return sm[:, 10 + y:11 + y]
        def c_bi2(y): return sm[:, 12 + y:13 + y]
        def c_max(y): return sm[:, 16 + 16 * y:32 + 16 * y]
        def c_sum(y): return sm[:, 48 + 16 * y:64 + 16 * y]
        def c_rcp(y): return sm[:, 80 + 16 * y:96 + 16 * y]
        GT0 = 112
        c_gred = sm[:, 144:145]
        c_gs = sm[0:1, 145:146]
        c_c1 = sm[0:1, 146:147]
        c128 = sm[:, 147:148]
        negc = sm[:, 148:149]
        bcolw = sm[:, 112:120]
        epsc = sm[:, 151:152]

        pt_state = [0]

        def trans_block(src_ap, ident, waits=()):
            b = pt_state[0] % 2
            pt_state[0] += 1
            ev = sd.op("tensor", "transpose", out=PT[b][:, 0:128], in_=src_ap,
                       identity=ident, waits=[need(("pt", b))] + list(waits),
                       inc="pe")
            return b, ev

        def evac_copy(b, ev_tr, dst_ap):
            e = sd.op("vector", "tensor_copy", out=dst_ap, in_=PT[b][:, 0:128],
                      waits=[ev_tr], inc="v")
            protect(("pt", b), e)
            return e

        def evac_split(b, ev_tr, dst_hi, dst_lo):
            sd.op("vector", "tensor_copy", out=dst_hi, in_=PT[b][:, 0:128],
                  waits=[ev_tr], inc="v")
            e = sd.op("vector", "tensor_tensor", out=dst_lo, in0=PT[b][:, 0:128],
                      in1=dst_hi, op=ALU.subtract, inc="v")
            protect(("pt", b), e)
            return e

        def trans_quad(srcs, ident, waits=()):
            b = pt_state[0] % 2
            pt_state[0] += 1
            ev = None
            for k, src in enumerate(srcs):
                w = [need(("pt", b))] + list(waits) if k == 0 else []
                ev = sd.op("tensor", "transpose", out=PT[b][:, k * 128:(k + 1) * 128],
                           in_=src, identity=ident, waits=w, inc="pe")
            return b, ev, len(srcs)

        def evac_quad_split(b, ev_tr, n, dst_hi, dst_lo):
            eh = sd.op("scalar", "copy", out=dst_hi, in_=PT[b][:, 0:n * 128],
                       waits=[ev_tr], inc="s")
            e = sd.op("vector", "tensor_tensor", out=dst_lo,
                      in0=PT[b][:, 0:n * 128], in1=dst_hi, op=ALU.subtract,
                      waits=[eh], inc="v")
            protect(("pt", b), e)
            return e

        def evac_quad_copy(b, ev_tr, n, dst_ap, eng="vector"):
            meth = "tensor_copy" if eng == "vector" else "copy"
            kw = dict(out=dst_ap, in_=PT[b][:, 0:n * 128])
            e = sd.op(eng, meth, waits=[ev_tr], inc="v" if eng == "vector" else "s", **kw)
            protect(("pt", b), e)
            return e

        ev_id = dma("sync", "lc", id32[:], ident_d, serial=True)
        sd.op("vector", "memset", ones16[:], 1.0)
        sd.op("vector", "memset", A_sb[:], 0.0)
        ev_eps = sd.op("vector", "memset", epsc, EPS, inc="v")

        ev_ga = dma("sync", "lc", AR[8][0:1, :], w_d["gamma"].unsqueeze(0),
                    serial=True)
        ev_be = dma("sync", "lc", AR[5][0:1, :], w_d["beta"].unsqueeze(0),
                    serial=True)
        ev_ones = sd.op("vector", "memset", ones32[:], 1.0, inc="v")
        bc_evs = []
        for srcrow, dst in ((AR[8][0:1, :], g128), (AR[5][0:1, :], be128)):
            for nh in range(2):
                evm = sd.op("tensor", "matmul", out=P_K[:, nh * 512:(nh + 1) * 512],
                            lhsT=ones32[0:1, 0:128],
                            rhs=srcrow[0:1, nh * 512:(nh + 1) * 512],
                            start=True, stop=True, skip_group_check=True,
                            waits=[ev_ones, ev_ga, ev_be, need("P_K")], inc="pe")
            evc = sd.op("vector", "tensor_copy", out=dst, in_=P_K[:, :],
                        waits=[evm], inc="v")
            protect("P_K", evc)
            bc_evs.append(evc)
        ev_g128, ev_b128 = bc_evs
        protect(("ar", 8), ev_g128)
        protect(("ar", 3), None)
        protect(("ar", 5), ev_b128)

        WSLOT = {"Wk": 0, "Wv": 2, "Wq": 4, "Wh": 6}
        RPART = {"Wq": 0, "Wk": 32, "Wv": 64, "Wh": 96}
        BIAS = {"Wq": "bq", "Wk": "bk", "Wv": "bv", "Wh": "bh"}

        last_gamma_use = None
        last_beta_use = None
        ev_wready = {}
        for wn in ("Wk", "Wv", "Wq", "Wh"):
            slot = WSLOT[wn]
            scaled = wn != "Wh"
            last = None
            last_bcol = None
            for oc in range(DC):
                wslot = 8 if oc % 2 == 0 else 3
                wb = AR[wslot]
                ev_l = dma("sync", f"lw{oc % 2}", wb[:, :],
                           w_d[wn][oc * P:(oc + 1) * P, :],
                           waits=[need(("ar", wslot))])
                if scaled:
                    evp = sd.op("vector", "tensor_tensor", out=AR[5][:, :],
                                in0=wb[:, :], in1=be128, op=ALU.mult,
                                waits=[ev_l, ev_b128], inc="v")
                    last_beta_use = evp
                    last_bcol = sd.op("vector", "reduce_sum",
                                      out=bcolw[:, oc:oc + 1], in_=AR[5][:, :],
                                      axis=AX.X, waits=[evp], inc="v")
                    ev_l = sd.op("vector", "tensor_tensor", out=wb[:, :],
                                 in0=wb[:, :], in1=g128, op=ALU.mult,
                                 waits=[ev_g128], inc="v")
                    last_gamma_use = ev_l
                for q in range(2):
                    srcs = [wb[:, (4 * q + k) * 128:(4 * q + k + 1) * 128]
                            for k in range(4)]
                    b, evt, n = trans_quad(srcs, id32[:], waits=[ev_l, ev_id])
                    awv = AW[:, :].rearrange("p (a b) -> p a b", b=1024)
                    hs = slot * 8 + 4 * q
                    ls = (slot + 1) * 8 + 4 * q
                    dh = awv[:, hs:hs + 4, oc * 128:oc * 128 + 128]
                    dl = awv[:, ls:ls + 4, oc * 128:oc * 128 + 128]
                    ps = PT[b][:, 0:512].rearrange("p (a b) -> p a b", a=4)
                    eh = sd.op("scalar", "copy", out=dh, in_=ps,
                               waits=[evt], inc="s")
                    last = sd.op("vector", "tensor_tensor", out=dl, in0=ps,
                                 in1=dh, op=ALU.subtract, waits=[eh], inc="v")
                    protect(("pt", b), last)
                protect(("ar", wslot), last)
            if scaled:
                bk = pt_state[0] % 2
                pt_state[0] += 1
                evt = sd.op("tensor", "transpose", out=PT[bk][0:8, 0:128],
                            in_=bcolw[:, 0:8], identity=id32[:],
                            waits=[need(("pt", bk)), ev_id, last_bcol], inc="pe")
                e1 = sd.op("vector", "tensor_copy", out=AR[0][0:8, 0:128],
                           in_=PT[bk][0:8, 0:128],
                           waits=[evt, need(("ar", 0))], inc="v")
                protect(("pt", bk), e1)
                e2 = dma("gpsimd", "sr", row_d.rearrange("(a b) -> a b", a=8),
                         AR[0][0:8, 0:128], waits=[e1, need("row_d")],
                         serial=True)
                protect(("ar", 0), e2)
                e3 = dma("gpsimd", "sr", AR[1][0:1, 0:1024],
                         row_d.unsqueeze(0), waits=[e2, need(("ar", 1))],
                         serial=True)
                protect("row_d", e3)
                ev_bl = dma("sync", "lc", AR[2][0:1, :],
                            w_d[BIAS[wn]].unsqueeze(0),
                            waits=[need(("ar", 2))], serial=True)
                ev_row = sd.op("vector", "tensor_tensor", out=AR[1][0:1, :],
                               in0=AR[1][0:1, :], in1=AR[2][0:1, :], op=ALU.add,
                               waits=[e3, ev_bl], inc="v")
                rowsrc = AR[1][0:1, :]
            else:
                ev_row = dma("sync", "lc", AR[2][0:1, :],
                             w_d[BIAS[wn]].unsqueeze(0),
                             waits=[need(("ar", 2))], serial=True)
                rowsrc = AR[2][0:1, :]
            eh = sd.op("vector", "tensor_copy", out=A16[0:1, 0:1024],
                       in_=rowsrc, waits=[ev_row, need("a16lo")], inc="v")
            el = sd.op("vector", "tensor_tensor", out=A16[0:1, 1024:2048],
                       in0=rowsrc, in1=A16[0:1, 0:1024], op=ALU.subtract,
                       inc="v")
            protect(("ar", 1), el)
            protect(("ar", 2), el)
            pw = RPART[wn]
            g1 = dma("gpsimd", "sr", biasrows[pw:pw + 1, 0:1024],
                     A16[0:1, 0:1024], waits=[eh], serial=True)
            g2 = dma("gpsimd", "sr", biasrows[pw:pw + 1, 1024:2048],
                     A16[0:1, 1024:2048], waits=[el], serial=True)
            protect("a16lo", g2)
            ev_wready[wn] = [last, g1, g2]

        protect(("ar4", 0), last_gamma_use)
        protect(("ar4", 1), last_beta_use)
        protect(("ar", 6), last_beta_use)
        protect(("ar", 7), last_beta_use)

        ev_b1c = dma("sync", "lc", b1c[:, :],
                     w_d["b1"].rearrange("(c p) -> p c", p=P), serial=True,
                     allow_slow_non_contiguous=True)
        ev_b12 = [ev_b1c]
        src = w_d["b2"].unsqueeze(0)
        ev_bl = dma("sync", "lc", AR[2][0:1, :], src,
                    waits=[need(("ar", 2))], serial=True)
        eh = sd.op("vector", "tensor_copy", out=A16[0:1, 0:1024],
                   in_=AR[2][0:1, :], waits=[ev_bl, need("a16lo")], inc="v")
        el = sd.op("vector", "tensor_tensor", out=A16[0:1, 1024:2048],
                   in0=AR[2][0:1, :], in1=A16[0:1, 0:1024], op=ALU.subtract,
                   inc="v")
        protect(("ar", 2), el)
        g1 = dma("gpsimd", "sr", biasrows[96:97, 2048:3072],
                 A16[0:1, 0:1024], waits=[eh], serial=True)
        g2 = dma("gpsimd", "sr", biasrows[96:97, 3072:4096],
                 A16[0:1, 1024:2048], waits=[el], serial=True)
        protect("a16lo", g2)
        ev_b12 += [g1, g2]

        def wsl(slot, kc, lo, n):
            base = slot * 8192 + kc * 1024 + lo
            return AW[:, base:base + n]

        def proj(dst_psum, lhs_hi, lhs_lo, wn, waits):
            slot = WSLOT[wn]
            pw = RPART[wn]
            last = None
            for nh in range(2):
                first = True
                for part, lhs in ((0, lhs_hi), (1, lhs_lo), (2, lhs_hi)):
                    wslot = slot + (1 if part == 2 else 0)
                    for kc in range(DC):
                        last = sd.op("tensor", "matmul",
                                     out=dst_psum[:, nh * 512:(nh + 1) * 512],
                                     lhsT=lhs[:, kc * 128:(kc + 1) * 128],
                                     rhs=wsl(wslot, kc, nh * 512, 512),
                                     start=first, stop=False,
                                     skip_group_check=True,
                                     waits=(waits if first else []), inc="pe")
                        first = False
                sd.op("tensor", "matmul", out=dst_psum[:, nh * 512:(nh + 1) * 512],
                      lhsT=ones16[pw:pw + 1, 0:128],
                      rhs=biasrows[pw:pw + 1, nh * 512:nh * 512 + 512],
                      start=False, stop=False, tile_position=(pw, 0),
                      skip_group_check=True, inc="pe")
                last = sd.op("tensor", "matmul",
                             out=dst_psum[:, nh * 512:(nh + 1) * 512],
                             lhsT=ones16[pw:pw + 1, 0:128],
                             rhs=biasrows[pw:pw + 1, 1024 + nh * 512:1536 + nh * 512],
                             start=False, stop=True, tile_position=(pw, 0),
                             skip_group_check=True, inc="pe")
            return last

        def softmax(psum, y, ebuf_ap, tag_waits):
            p3 = psum[:, :].rearrange("p (h w) -> p h w", h=H)
            e3 = ebuf_ap.rearrange("p (h w) -> p h w", h=H)
            ev_m = sd.op("vector", "reduce_max", out=c_max(y), in_=p3, axis=AX.X,
                         waits=tag_waits, inc="v")
            ev_sub = sd.op("vector", "tensor_tensor", out=e3, in0=p3,
                           in1=c_max(y).unsqueeze(2).broadcast_to([P, H, HD]),
                           op=ALU.subtract,
                           waits=[ev_m, need(("ar", 6 + y))], inc="v")
            ev_exp = sd.op("scalar", "activation", out=ebuf_ap, in_=ebuf_ap,
                           func=AF.Exp, waits=[ev_sub], inc="s")
            ev_sm = sd.op("vector", "reduce_sum", out=c_sum(y), in_=e3, axis=AX.X,
                          waits=[ev_exp], inc="v")
            ev_r = sd.op("vector", "reciprocal", out=c_rcp(y), in_=c_sum(y),
                         waits=[ev_sm], inc="v")
            return ev_exp, ev_sub, ev_r

        def ln_finalize(y, ev_1, ev_2, dst_sc, dst_bi):
            ev_mu = sd.op("vector", "tensor_scalar", out=c_mu(y), in0=c_s1(y),
                          scalar1=1.0 / D, scalar2=None, op0=ALU.mult,
                          waits=[ev_1], inc="v")
            ev_ms = sd.op("vector", "tensor_tensor", out=c_msq(y), in0=c_mu(y),
                          in1=c_mu(y), op=ALU.mult, inc="v")
            ev_va = sd.op("vector", "scalar_tensor_tensor", out=c_var(y),
                          in0=c_s2(y), scalar=1.0 / D, in1=c_msq(y),
                          op0=ALU.mult, op1=ALU.subtract, waits=[ev_2, ev_ms],
                          inc="v")
            ev_sd = sd.op("scalar", "activation", out=c_msq(y), in_=c_var(y),
                          func=AF.Sqrt, bias=epsc, scale=1.0, waits=[ev_va],
                          inc="s")
            ev_rs = sd.op("vector", "reciprocal", out=dst_sc, in_=c_msq(y),
                          waits=[ev_sd], inc="v")
            ev_bi = sd.op("vector", "scalar_tensor_tensor", out=dst_bi,
                          in0=c_mu(y), scalar=-1.0, in1=dst_sc, op0=ALU.mult,
                          op1=ALU.mult, waits=[ev_rs], inc="v")
            return ev_bi

        def ln_stats(y, src_ap, tmp_ap, dst_sc, dst_bi, waits):
            ev_1 = sd.op("vector", "reduce_sum", out=c_s1(y), in_=src_ap,
                         axis=AX.X, waits=waits + [need(("s1", y))], inc="v")
            ev_2 = sd.op("vector", "scalar_tensor_tensor", out=tmp_ap,
                         in0=src_ap, scalar=1.0, in1=src_ap, op0=ALU.mult,
                         op1=ALU.mult, accum_out=c_s2(y),
                         waits=waits + [need(("ar4", y))], inc="v")
            ev_bi = ln_finalize(y, ev_1, ev_2, dst_sc, dst_bi)
            protect(("s1", y), ev_bi)
            protect(("ar4", y), ev_2)
            return ev_bi, ev_2

        evs = {}

        gacc = {"W1": sm[:, GT0 + 0:GT0 + 1], "W2": sm[:, GT0 + 1:GT0 + 2]}
        gtmp2 = sm[:, GT0 + 2:GT0 + 3]
        ev_gz = sd.op("vector", "memset", sm[:, GT0:GT0 + 2], 0.0, inc="v")

        def gamma_unit(wname, tile_ap):
            ev_l = dma("gpsimd", "lg", AR[8][:, :], tile_ap,
                       waits=[need(("ar", 8))])
            ev_r = sd.op("vector", "tensor_reduce", out=gtmp2, in_=AR[8][:, :],
                         axis=AX.X, op=ALU.add, apply_absolute_value=True,
                         waits=[ev_l, need("gtmp2")], inc="v")
            protect(("ar", 8), ev_r)
            ev_a = sd.op("vector", "tensor_tensor", out=gacc[wname],
                         in0=gacc[wname], in1=gtmp2, op=ALU.add,
                         waits=[ev_r, ev_gz], inc="v")
            protect("gtmp2", ev_a)
            return ev_a

        prep_gammas = ([("gW1", t) for t in range(FCN)] +
                       [("gW2", t) for t in range(FCN)])

        def run_gamma(u):
            kind, t = u
            if kind == "gW1":
                ev = gamma_unit("W1", w_d["W1"][t * P:(t + 1) * P, :])
                if t == FCN - 1:
                    protect("gacc_W1", ev)
            else:
                ev = gamma_unit("W2", w_d["W2"][(t // 4) * P:(t // 4 + 1) * P,
                                                (t % 4) * 1024:(t % 4 + 1) * 1024])
                if t == FCN - 1:
                    protect("gacc_W2", ev)

        def prep_gamma_unit(j):
            per = max(1, (2 * FCN + NT - 1) // NT)
            for _ in range(per):
                if prep_gammas:
                    run_gamma(prep_gammas.pop(0))

        sd.op("vector", "engine_nop", waits=[("sr", sd.cnt["sr"]),
                                             ("lc", sd.cnt["lc"])])
        def emitA1(i):
            y = i % 2
            b, r = i // TPB, (i % TPB) * P
            ev_x = dma("sync", f"lx{y}", AR[y][:, :], x_d[b, r:r + P, :],
                       waits=[need(("ar", y))])
            evs["x", i] = ev_x
            ev_bi, _ = ln_stats(y, AR[y][:, :], AR[4 + y][:, :],
                                st_sc[:, i:i + 1], st_bi[:, i:i + 1], [ev_x])
            ev_nx = sd.op("vector", "tensor_scalar", out=AR[2 + y][:, :],
                          in0=AR[y][:, :], scalar1=st_sc[:, i:i + 1],
                          scalar2=st_bi[:, i:i + 1], op0=ALU.mult, op1=ALU.add,
                          waits=[ev_bi, need(("ar", 2 + y))], inc="v")
            evs["nx", i] = ev_nx
            protect(("ar", y), ev_nx)

        def trans_nx1_pe(i):
            y = i % 2
            quads = []
            for q in range(2):
                srcs = [AR[2 + y][:, (4 * q + k) * 128:(4 * q + k + 1) * 128]
                        for k in range(4)]
                b, evt, n = trans_quad(srcs, id32[:], waits=[evs["nx", i]])
                quads.append((b, evt, n, q))
            return quads

        def trans_nx1_evac(i, quads):
            last = None
            for b, evt, n, q in quads:
                last = evac_quad_split(b, evt, n,
                                       nxT_hi[:, q * 512:(q + 1) * 512],
                                       nxT_lo[:, q * 512:(q + 1) * 512])
            evs["nxT", i] = last

        def trans_nx1(i):
            trans_nx1_evac(i, trans_nx1_pe(i))

        def emitB1(j):
            y = j % 2
            b, tib = j // TPB, j % TPB
            ev_k = proj(P_K, nxT_hi, nxT_lo, "Wk",
                        [evs["nxT", j]] + ev_wready["Wk"] + [need("P_K")])
            ev_v = proj(P_V, nxT_hi, nxT_lo, "Wv",
                        ev_wready["Wv"] + [need("P_V")])
            quads = trans_nx1_pe(j + 1) if j + 1 < NT else None
            ev_exp, ev_sub, ev_r = softmax(P_K, y, AR[6 + y][:, :], [ev_k])
            protect("P_K", ev_sub)
            ev_vp = sd.op("vector", "tensor_tensor",
                          out=AR[2 + y][:, :].rearrange("p (h w) -> p h w", h=H),
                          in0=P_V[:, :].rearrange("p (h w) -> p h w", h=H),
                          in1=c_rcp(y).unsqueeze(2).broadcast_to([P, H, HD]),
                          op=ALU.mult, waits=[ev_v, ev_r], inc="v")
            protect("P_V", ev_vp)
            ek16 = B16[:, 0:2048]
            v16 = B16[:, 2048:4096]
            ev_eh = sd.op("scalar", "copy", out=ek16[:, 0:1024],
                          in_=AR[6 + y][:, :],
                          waits=[ev_exp, need(("b16", 0))], inc="s")
            ev_el = sd.op("vector", "tensor_tensor", out=ek16[:, 1024:2048],
                          in0=AR[6 + y][:, :], in1=ek16[:, 0:1024],
                          op=ALU.subtract, waits=[ev_eh], inc="v")
            ev_vh = sd.op("scalar", "copy", out=v16[:, 0:1024],
                          in_=AR[2 + y][:, :],
                          waits=[ev_vp, need(("b16", 1))], inc="s")
            ev_vl = sd.op("vector", "tensor_tensor", out=v16[:, 1024:2048],
                          in0=AR[2 + y][:, :], in1=v16[:, 0:1024],
                          op=ALU.subtract, waits=[ev_vh], inc="v")
            if quads is not None:
                trans_nx1_evac(j + 1, quads)
            ev_A = None
            for h in range(H):
                hp = (h % 2) * 64
                for pi, (lo, ro) in enumerate(((0, 0), (1024, 0), (0, 1024))):
                    w = [ev_el, ev_vl]
                    if tib == 0 and h == 0 and pi == 0:
                        w += [need("P_A")]
                    ev_A = sd.op("tensor", "matmul",
                                 out=P_A[hp:hp + 64,
                                         (h // 2) * 64:(h // 2) * 64 + 64],
                                 lhsT=ek16[:, lo + h * 64:lo + (h + 1) * 64],
                                 rhs=v16[:, ro + h * 64:ro + (h + 1) * 64],
                                 start=(tib == 0 and h <= 1 and pi == 0),
                                 stop=(tib == TPB - 1 and h == H - 1 and pi == 2),
                                 tile_position=(0, hp), skip_group_check=True,
                                 waits=w, inc="pe")
            protect(("ar", 6 + y), ev_A)
            protect(("ar", 2 + y), ev_A)
            protect(("b16", 0), ev_A)
            protect(("b16", 1), ev_A)
            if tib == TPB - 1:
                e = None
                for h in range(H):
                    hp = (h % 2) * 64
                    c = h // 2
                    src = P_A[hp:hp + 64, c * 64:c * 64 + 64]
                    dhi = A_sb[hp:hp + 64,
                               b * 2048 + c * 128 + hp:b * 2048 + c * 128 + hp + 64]
                    dlo = A_sb[hp:hp + 64,
                               b * 2048 + 1024 + c * 128 + hp:
                               b * 2048 + 1024 + c * 128 + hp + 64]
                    eh = sd.op("vector", "tensor_copy", out=dhi, in_=src,
                               waits=[ev_A], inc="v")
                    e = sd.op("vector", "tensor_tensor", out=dlo, in0=src,
                              in1=dhi, op=ALU.subtract, inc="v")
                protect("P_A", e)
                evs["Asb", b] = e

        emitA1(0)
        trans_nx1(0)
        for j in range(NT):
            if j + 1 < NT:
                emitA1(j + 1)
            emitB1(j)
            prep_gamma_unit(j)

        def emitA2(i):
            y = i % 2
            b, r = i // TPB, (i % TPB) * P
            ev_x = dma("sync", f"lx{y}", AR[y][:, :], x_d[b, r:r + P, :],
                       waits=[need(("ar", y))])
            evs["x", i] = ev_x
            ev_nx = sd.op("vector", "tensor_scalar", out=AR[2 + y][:, :],
                          in0=AR[y][:, :], scalar1=st_sc[:, i:i + 1],
                          scalar2=st_bi[:, i:i + 1], op0=ALU.mult, op1=ALU.add,
                          waits=[ev_x, need(("ar", 2 + y))], inc="v")
            evs["nx", i] = ev_nx

        def trans_pair2(i):
            y = i % 2
            last = None
            for src, dhi, dlo, sev in ((AR[2 + y], nxT_hi, nxT_lo, evs["nx", i]),
                                       (AR[y], xT_hi, xT_lo, evs["x", i])):
                for q in range(2):
                    srcs = [src[:, (4 * q + k) * 128:(4 * q + k + 1) * 128]
                            for k in range(4)]
                    b, evt, n = trans_quad(srcs, id32[:], waits=[sev])
                    last = evac_quad_split(b, evt, n,
                                           dhi[:, q * 512:(q + 1) * 512],
                                           dlo[:, q * 512:(q + 1) * 512])
            evs["xT", i] = last
            protect(("ar", 2 + y), last)

        def emitB2(j):
            y = j % 2
            b = j // TPB
            ev_q = proj(P_K, nxT_hi, nxT_lo, "Wq",
                        [evs["xT", j]] + ev_wready["Wq"] + [need("P_K")])
            ev_h = proj(P_V, xT_hi, xT_lo, "Wh",
                        ev_wready["Wh"] + [need("P_V")])
            ev_exp, ev_sub, ev_r = softmax(P_K, y, AR[6 + y][:, :], [ev_q])
            protect("P_K", ev_sub)
            ev_qsm = sd.op("vector", "tensor_tensor",
                           out=AR[4 + y][:, :].rearrange("p (h w) -> p h w", h=H),
                           in0=AR[6 + y][:, :].rearrange("p (h w) -> p h w", h=H),
                           in1=c_rcp(y).unsqueeze(2).broadcast_to([P, H, HD]),
                           op=ALU.mult,
                           waits=[ev_exp, ev_r, need(("ar4", y))], inc="v")
            protect(("ar", 6 + y), ev_qsm)
            q16 = B16[:, 0:2048]
            last_ev = None
            evt = None
            for q in range(2):
                srcs = [AR[4 + y][:, (4 * q + k) * 128:(4 * q + k + 1) * 128]
                        for k in range(4)]
                bk, evt, n = trans_quad(srcs, id32[:],
                                        waits=[ev_qsm] + ([need(("b16", 0))] if q == 0 else []))
                last_ev = evac_quad_split(bk, evt, n,
                                          q16[:, q * 512:(q + 1) * 512],
                                          q16[:, 1024 + q * 512:1024 + (q + 1) * 512])
            if j + 1 < NT:
                trans_pair2(j + 1)
            ev_y = None
            for c in range(DC):
                for lo, ro in ((0, 0), (1024, 0), (0, 1024)):
                    ev_y = sd.op("tensor", "matmul",
                                 out=P_V[:, c * 128:(c + 1) * 128],
                                 lhsT=q16[:, lo + c * 128:lo + c * 128 + 128],
                                 rhs=A_sb[:, b * 2048 + ro + c * 128:
                                          b * 2048 + ro + c * 128 + 128],
                                 start=False, stop=(c == DC - 1 and ro == 1024),
                                 skip_group_check=True,
                                 waits=[last_ev, ev_h, evs["Asb", b]], inc="pe")
            protect(("b16", 0), ev_y)
            ev_r1 = sd.op("vector", "scalar_tensor_tensor", out=AR[y][:, :],
                          in0=AR[y][:, :], scalar=1.0, in1=P_V[:, :],
                          op0=ALU.mult, op1=ALU.add, accum_out=c_s1(y),
                          waits=[ev_y, need(("s1", y))], inc="v")
            protect("P_V", ev_r1)
            ev_2 = sd.op("scalar", "activation", out=AR[4 + y][:, :],
                         in_=AR[y][:, :], func=AF.Square, accum_out=c_s2(y),
                         waits=[ev_r1, evt, need(("s1", y))], inc="s")
            ev_bi = ln_finalize(y, ev_r1, ev_2, c_rs2(y), c_bi2(y))
            protect(("s1", y), ev_bi)
            ev_x2 = sd.op("scalar", "activation", out=AR[4 + y][:, :],
                          in_=AR[y][:, :], func=AF.Identity, scale=c_rs2(y),
                          bias=c_bi2(y), waits=[ev_bi, ev_2], inc="s")
            protect(("ar", y), ev_x2)
            bb, r = j // TPB, (j % TPB) * P
            ev_st = dma("gpsimd", f"sx{y}", x2_d[bb, r:r + P, :],
                        AR[4 + y][:, :], waits=[ev_x2])
            evs["x2st", j] = ev_st
            protect(("ar4", y), ev_st)

        emitA2(0)
        trans_pair2(0)
        for j in range(NT):
            if j + 1 < NT:
                emitA2(j + 1)
            emitB2(j)

        evP2 = ("pe", sd.cnt["pe"])
        protect(("ar", 8), [evP2])
        protect(("ar", 3), [evP2])
        protect("a16lo", [evP2])
        protect("a16wq", [evP2])
        protect(("PO", 0), (need("P_K") or []))
        protect(("PO", 1), (need("P_V") or []))

        def ffn_gamma(wname):
            while prep_gammas:
                run_gamma(prep_gammas.pop(0))
            ev_mm = sd.op("tensor", "transpose", out=P_T0[0:1, 0:128],
                          in_=gacc[wname], identity=id32[:],
                          waits=[need("gacc_" + wname), need(("pt", 0)), ev_id], inc="pe")
            ev_cp = sd.op("vector", "reduce_sum", out=c_gs,
                          in_=P_T0[0:1, 0:128], axis=AX.X,
                          waits=[ev_mm], inc="v")
            protect(("pt", 0), ev_cp)
            ev_c1 = sd.op("vector", "tensor_scalar", out=c_c1, in0=c_gs,
                          scalar1=0.5 / (FF * D), scalar2=0.5 * EPS,
                          op0=ALU.mult, op1=ALU.add, waits=[ev_cp], inc="v")
            ev_h16 = sd.op("vector", "tensor_copy", out=cb16[0:1, 0:1],
                           in_=c_c1, waits=[ev_c1], inc="v")
            ev_d1 = sd.op("vector", "tensor_tensor", out=sm[0:1, 150:151],
                          in0=c_c1, in1=cb16[0:1, 0:1], op=ALU.subtract,
                          waits=[ev_h16], inc="v")
            ev_l16 = sd.op("vector", "tensor_copy", out=cb16[0:1, 1:2],
                           in_=sm[0:1, 150:151], waits=[ev_d1], inc="v")
            ev_ll = sd.op("vector", "tensor_tensor", out=cb16[0:1, 2:3],
                          in0=sm[0:1, 150:151], in1=cb16[0:1, 1:2],
                          op=ALU.subtract, waits=[ev_l16], inc="v")
            sd.op("tensor", "matmul", out=P_SP[:, 0:1],
                  lhsT=ones16[0:1, 0:128], rhs=cb16[0:1, 0:1], start=True,
                  stop=False, skip_group_check=True,
                  waits=[ev_ll, need("P_SPbc")], inc="pe")
            sd.op("tensor", "matmul", out=P_SP[:, 0:1],
                  lhsT=ones16[0:1, 0:128], rhs=cb16[0:1, 1:2],
                  start=False, stop=False, skip_group_check=True, inc="pe")
            ev_bm = sd.op("tensor", "matmul", out=P_SP[:, 0:1],
                          lhsT=ones16[0:1, 0:128], rhs=cb16[0:1, 2:3],
                          start=False, stop=True, skip_group_check=True,
                          inc="pe")
            ev_bc = sd.op("vector", "tensor_copy", out=c128, in_=P_SP[:, 0:1],
                          waits=[ev_bm], inc="v")
            protect("P_SPbc", ev_bc)
            ev_ng = sd.op("vector", "tensor_scalar", out=negc, in0=c128,
                          scalar1=-1.0, scalar2=None, op0=ALU.mult,
                          waits=[ev_bc], inc="v")
            return [ev_bc, ev_ng]

        def ffn_quant(ntiles, tile_ap, dst_quad, ev_c):
            last = None
            for t in range(ntiles):
                wslot = 8 if t % 2 == 0 else 3
                wb = AR[wslot]
                ev_l = dma("sync", f"lw{t % 2}", wb[:, :], tile_ap(t),
                           waits=[need(("ar", wslot))])
                ev_pa = sd.op("vector", "tensor_scalar", out=AR[5][:, :],
                              in0=wb[:, :], scalar1=c128[:, 0:1], scalar2=None,
                              op0=ALU.is_gt,
                              waits=[ev_l, need(("ar4", 1)), need(("ar", 5))] + ev_c,
                              inc="v")
                ev_pb = sd.op("vector", "tensor_scalar", out=AR[6][:, :],
                              in0=wb[:, :], scalar1=negc[:, 0:1], scalar2=None,
                              op0=ALU.is_lt, waits=[need(("ar", 6))], inc="v")
                ev_wq = sd.op("vector", "tensor_tensor", out=AR[7][:, :],
                              in0=AR[5][:, :], in1=AR[6][:, :],
                              op=ALU.subtract,
                              waits=[ev_pb, need(("ar", 7))], inc="v")
                protect(("ar", wslot), ev_wq)
                protect(("ar4", 1), ev_wq)
                protect(("ar", 5), ev_wq)
                protect(("ar", 6), ev_wq)
                for q in range(2):
                    srcs = [AR[7][:, (4 * q + k) * 128:(4 * q + k + 1) * 128]
                            for k in range(4)]
                    bk, evt, n = trans_quad(srcs, id32[:], waits=[ev_wq])
                    last = sd.op("vector", "tensor_copy",
                                 out=dst_quad(t, q),
                                 in_=PT[bk][:, 0:512].rearrange("p (a b) -> p a b", a=4),
                                 waits=[evt], inc="v")
                    protect(("pt", bk), last)
                protect(("ar", 7), last)
            return last

        ev_c1l = ffn_gamma("W1")
        def w1_dst_quad(t, q):
            awv = AW[:, :].rearrange("p (a b) -> p a b", b=4096)
            return awv[:, 4 * q:4 * q + 4, t * 128:t * 128 + 128]

        ev_w1 = ffn_quant(FCN, lambda t: w_d["W1"][t * P:(t + 1) * P, :],
                          w1_dst_quad, ev_c1l)
        ev_c2l = ffn_gamma("W2")
        def w2_dst_quad(t, q):
            awv = AW[:, :].rearrange("p (a b) -> p a b", b=1024)
            si = 32 + (t % 4) * 8 + 4 * q
            return awv[:, si:si + 4, (t // 4) * 128:(t // 4) * 128 + 128]

        ev_w2 = ffn_quant(FCN,
                          lambda t: w_d["W2"][(t // 4) * P:(t // 4 + 1) * P,
                                              (t % 4) * 1024:(t % 4 + 1) * 1024],
                          w2_dst_quad, ev_c2l)
        protect(("hT32", 0), (need(("ar", 8)) or []))
        protect(("hT32", 1), (need(("ar", 8)) or []))
        protect(("hT16", 0), (need(("b16", 1)) or []) + (need(("b16", 0)) or []))
        protect(("hT16", 1), (need(("b16", 1)) or []) + (need(("b16", 0)) or []))

        def emitA3(s):
            p = s % 2
            for u in range(2):
                j = 2 * s + u
                b, r = j // TPB, (j % TPB) * P
                ev = dma("sync", f"l2{2 * p + u}", AR[2 * p + u][:, :],
                         x2_d[b, r:r + P, :],
                         waits=[evs["x2st", j], need(("ar", 2 * p + u))])
                evs["x2ld", j] = ev

        def trans3(s):
            p = s % 2
            xa, xb = AR[2 * p], AR[2 * p + 1]
            last = None
            for q in range(4):
                srcs = []
                for c in (2 * q, 2 * q + 1):
                    for src in (xa, xb):
                        srcs.append(src[:, c * 128:(c + 1) * 128])
                bb_ = pt_state[0] % 2
                bk, evt, n = trans_quad(srcs, id32[:],
                                        waits=[evs["x2ld", 2 * s],
                                               evs["x2ld", 2 * s + 1],
                                               need(("g", 2 + bb_))])
                last = evac_quad_split(bk, evt, n,
                                       x2T_hi[:, q * 512:(q + 1) * 512],
                                       x2T_lo[:, q * 512:(q + 1) * 512])
                protect(("g", 2 + bk), last)
            evs["x2T", s] = last

        def emit_g(s, fc):
            r = fc % 4
            gp = GSL[r][:, 0:256]
            first = True
            ev = None
            for kc in range(DC):
                for x2t in (x2T_hi, x2T_lo):
                    w = []
                    if first:
                        w = [evs["x2T", s], ev_w1, ev_w2,
                             need(("g", r))] + list(ev_b12)
                    ev = sd.op("tensor", "matmul", out=gp,
                               lhsT=AW[:, kc * 4096 + fc * 128:
                                       kc * 4096 + fc * 128 + 128],
                               rhs=x2t[:, kc * 256:(kc + 1) * 256],
                               start=first,
                               stop=(kc == DC - 1 and x2t is x2T_lo),
                               skip_group_check=True, waits=w, inc="pe")
                    first = False
            evs["g", fc] = ev

        def emit_relu_split(fc):
            r = fc % 4
            ev_rel = sd.op("scalar", "activation",
                           out=AR[8][:, 256 * r:256 * r + 256],
                           in_=GSL[r][:, 0:256], func=AF.Relu,
                           bias=b1c[:, fc:fc + 1], scale=1.0,
                           waits=[evs["g", fc], need(("hT32", r))], inc="s")
            protect(("g", r), ev_rel)
            if r >= 2:
                protect(("pt", r - 2), ev_rel)
            ev_hi = sd.op("scalar", "copy", out=hT_hi(r),
                          in_=AR[8][:, 256 * r:256 * r + 256],
                          waits=[ev_rel, need(("hT16", r))], inc="s")
            ev_lo = sd.op("vector", "tensor_tensor", out=hT_lo(r),
                          in0=AR[8][:, 256 * r:256 * r + 256], in1=hT_hi(r),
                          op=ALU.subtract, waits=[ev_hi], inc="v")
            protect(("hT32", r), ev_lo)
            evs["hT", fc] = ev_lo

        def emit_ffn2(fc):
            r = fc % 4
            ev = None
            first = True
            for sub in range(2):
                for ring in (hT_hi(r), hT_lo(r)):
                    for nh in range(2):
                        ev = sd.op("tensor", "matmul",
                                   out=P_O[sub][:, nh * 512:(nh + 1) * 512],
                                   lhsT=ring[:, sub * 128:sub * 128 + 128],
                                   rhs=AW[:, 32768 + fc * 1024 + nh * 512:
                                          32768 + fc * 1024 + nh * 512 + 512],
                                   start=False, stop=(fc == FCN - 1),
                                   skip_group_check=True,
                                   waits=[evs["hT", fc]] if first else [],
                                   inc="pe")
                        first = False
            protect(("hT16", r), ev)

        def fcloop(s):
            emit_g(s, 0)
            emit_relu_split(0)
            emit_g(s, 1)
            emit_g(s, 2)
            emit_g(s, 3)
            for sub in range(2):
                for nh in range(2):
                    sd.op("tensor", "matmul",
                          out=P_O[sub][:, nh * 512:(nh + 1) * 512],
                          lhsT=ones16[96:97, 0:128],
                          rhs=biasrows[96:97, 2048 + nh * 512:2048 + nh * 512 + 512],
                          start=True, stop=False, tile_position=(96, 0),
                          skip_group_check=True,
                          waits=[need(("PO", sub))] + list(ev_b12), inc="pe")
                    sd.op("tensor", "matmul",
                          out=P_O[sub][:, nh * 512:(nh + 1) * 512],
                          lhsT=ones16[96:97, 0:128],
                          rhs=biasrows[96:97, 3072 + nh * 512:3584 + nh * 512],
                          start=False, stop=False, tile_position=(96, 0),
                          skip_group_check=True, inc="pe")
            for fc in range(FCN):
                if fc >= 1:
                    emit_relu_split(fc)
                emit_ffn2(fc)
                if fc + 4 < FCN:
                    emit_g(s, fc + 4)
            evs["ffn2end", s] = ("pe", sd.cnt["pe"])

        def epi(s):
            p = s % 2
            xa, xb = AR[2 * p], AR[2 * p + 1]
            for sub in range(2):
                xs = (xa, xb)[sub]
                j = 2 * s + sub
                y = sub
                ev_r2 = sd.op("vector", "scalar_tensor_tensor", out=xs[:, :],
                              in0=xs[:, :], scalar=1.0, in1=P_O[sub][:, :],
                              op0=ALU.mult, op1=ALU.add, accum_out=c_s1(y),
                              waits=[evs["ffn2end", s], need(("s1", y))],
                              inc="v")
                protect(("PO", sub), ev_r2)
                ev_2 = sd.op("scalar", "activation", out=AR[4 + y][:, :],
                             in_=xs[:, :], func=AF.Square, accum_out=c_s2(y),
                             waits=[ev_r2, need(("ar4", y)),
                                    need(("s1", y))], inc="s")
                ev_bi = ln_finalize(y, ev_r2, ev_2, c_rs2(y), c_bi2(y))
                protect(("s1", y), ev_bi)
                protect(("ar4", y), ev_2)
                ev_o = sd.op("scalar", "activation", out=AR[6 + sub][:, :],
                             in_=xs[:, :], func=AF.Identity, scale=c_rs2(y),
                             bias=c_bi2(y),
                             waits=[ev_bi, need(("ar", 6 + sub))], inc="s")
                protect(("ar", 2 * p + sub), ev_o)
                b, r = j // TPB, (j % TPB) * P
                ev_os = dma("gpsimd", f"so{sub}", out_d[b, r:r + P, :],
                            AR[6 + sub][:, :], waits=[ev_o])
                protect(("ar", 6 + sub), ev_os)

        emitA3(0)
        trans3(0)
        for s3 in range(NSUP):
            fcloop(s3)
            if s3 + 1 < NSUP:
                emitA3(s3 + 1)
                trans3(s3 + 1)
            epi(s3)

        sd.op("gpsimd", "nop",
              waits=[(k, sd.cnt[k]) for k in ("sr", "sx0", "sx1", "so0", "so1")])

        with nc.Block() as block:
            @block.sync
            def _(h):
                sd.replay("sync", h, sems)

            @block.gpsimd
            def _(h):
                sd.replay("gpsimd", h, sems)

            @block.vector
            def _(h):
                sd.replay("vector", h, sems)

            @block.scalar
            def _(h):
                sd.replay("scalar", h, sems)

            @block.tensor
            def _(h):
                sd.replay("tensor", h, sems)

    return nc


def kernel(L0_attn_x, L0_attn_Wq, L0_attn_bq, L0_attn_Wk, L0_attn_bk,
           L0_attn_Wv, L0_attn_bv, L0_attn_Wh, L0_attn_bh, L0_attn_gamma,
           L0_attn_beta, L0_ffn_W1, L0_ffn_b1, L0_ffn_W2, L0_ffn_b2):
    x = np.ascontiguousarray(np.asarray(L0_attn_x, dtype=np.float32))
    NB, T, Dm = x.shape
    ncores = 8
    Bc = NB // ncores
    common = {
        "Wq": np.ascontiguousarray(L0_attn_Wq, dtype=np.float32),
        "bq": np.ascontiguousarray(L0_attn_bq, dtype=np.float32),
        "Wk": np.ascontiguousarray(L0_attn_Wk, dtype=np.float32),
        "bk": np.ascontiguousarray(L0_attn_bk, dtype=np.float32),
        "Wv": np.ascontiguousarray(L0_attn_Wv, dtype=np.float32),
        "bv": np.ascontiguousarray(L0_attn_bv, dtype=np.float32),
        "Wh": np.ascontiguousarray(L0_attn_Wh, dtype=np.float32),
        "bh": np.ascontiguousarray(L0_attn_bh, dtype=np.float32),
        "gamma": np.ascontiguousarray(L0_attn_gamma, dtype=np.float32),
        "beta": np.ascontiguousarray(L0_attn_beta, dtype=np.float32),
        "W1": np.ascontiguousarray(L0_ffn_W1, dtype=np.float32),
        "b1": np.ascontiguousarray(L0_ffn_b1, dtype=np.float32),
        "W2": np.ascontiguousarray(L0_ffn_W2, dtype=np.float32),
        "b2": np.ascontiguousarray(L0_ffn_b2, dtype=np.float32),
        "ident": np.eye(128, dtype=np.float32),
    }
    nc = build_nc(Bc, T)
    in_maps = [dict(common, x=np.ascontiguousarray(x[c * Bc:(c + 1) * Bc]))
               for c in range(ncores)]
    res = run_bass_kernel_spmd(nc, in_maps, core_ids=list(range(ncores)))
    return np.concatenate([r["out"] for r in res.results], axis=0)
